# version 03
# baseline (speedup 1.0000x reference)
import os as _os
_os.environ.setdefault("GAUGE_NTFF_IGNORE_DMA", "1")
"""Development version of the megakernel. Self-contained infra + phase builders.

Layout conventions:
- Activations feature-major: SBUF tile [128 partitions = d-chunk, cols = chunk*T + t].
- matmul: out[M,N] = lhsT[K,M].T @ rhs[K,N]; lhsT = weight chunk [128,<=128] bf16,
  rhs = activation [128, <=512] bf16, out = PSUM fp32.
- Weights in HBM pre-arranged host-side as [din, dout] (math layout); streamed with
  SWDGE cast-DMA (fp32 HBM -> bf16 SBUF).

Synchronization model (sound w.r.t. the bass race detector):
- Per engine stream: one compute sem, cumulative +1 on ops that are dependency
  targets (engine completion is in-order FIFO, so cumulative waits are sound).
- Same-engine compute->compute edges DO get semaphores (device requirement),
  except TensorE->TensorE (PE FIFO is safe).
- DMA queues: completions are NOT ordered within a queue, so each queue uses a
  rotating pool of R sems; dma #n uses sem n%R and the ISSUER pre-waits for
  dma #(n-R) completion before issuing #n.  Hence "#n complete" implies "all
  #<=n-R complete", which the vector clocks exploit.
"""
import numpy as np
import concourse.bass as bass
import concourse.mybir as mybir

F32 = mybir.dt.float32
BF16 = mybir.dt.bfloat16
AF = mybir.ActivationFunctionType
ALU = mybir.AluOpType
AX = mybir.AxisListType
ds = bass.ds

ENGINES = ("ten", "dve", "act", "pool", "sp")
ENG_ATTR = {"ten": "tensor", "dve": "vector", "act": "scalar", "pool": "gpsimd", "sp": "sync"}
R_DMA = 6

ALL = "*"


class Op:
    __slots__ = ("stream", "idx", "calls", "is_dma", "inc_needed", "vclock", "waits",
                 "tag", "sem_val")

    def __init__(self, stream, idx, calls, is_dma, tag):
        self.stream = stream
        self.idx = idx
        self.calls = calls
        self.is_dma = is_dma
        self.inc_needed = is_dma
        self.vclock = None
        self.waits = []
        self.tag = tag
        self.sem_val = None


def _dma_cnt(m, j):
    if m < 0 or m < j:
        return 0
    return (m - j) // R_DMA + 1


class Stream:
    def __init__(self, name):
        self.name = name
        self.ops = []
        self.n_comp = 0
        self.n_dma = 0
        self.clock = {}

    def ckey(self):
        return (self.name, "c")

    def dkey(self, j):
        return (self.name, "d", j)


class Prog:
    def __init__(self, nc):
        self.nc = nc
        self.streams = {n: Stream(n) for n in ENGINES}
        self.state = {}
        self.n_ops = 0

    def _region(self, r):
        return self.state.setdefault(r, {})

    def clear_region(self, region):
        st = self._region(region)
        pend = []
        for w, rd in st.values():
            if w is not None:
                pend.append(w)
            pend.extend(rd)
        self.state[region] = {"__pending__": (None, pend)}

    def op(self, stream, calls, reads=(), writes=(), is_dma=False, tag=""):
        s = self.streams[stream]
        if not isinstance(calls, (list, tuple)):
            calls = [calls]

        def norm(lst):
            out = []
            for x in lst:
                if isinstance(x, tuple) and len(x) == 2:
                    out.append(x)
                else:
                    out.append((x, ALL))
            return out

        reads = norm(list(reads))
        writes = norm(list(writes))

        idx = s.n_dma if is_dma else s.n_comp
        op = Op(s, idx, calls, is_dma, tag)

        deps = []

        def add_dep(d):
            if d is not None and d is not op:
                deps.append(d)

        for (rg, sub) in reads:
            st = self._region(rg)
            if sub == ALL:
                for k, (w, rd) in st.items():
                    add_dep(w)
                    if k == "__pending__":
                        for d in rd:
                            add_dep(d)
            else:
                for k in (sub, ALL):
                    w, rd = st.get(k, (None, []))
                    add_dep(w)
                pw, prd = st.get("__pending__", (None, []))
                for d in prd:
                    add_dep(d)
        for (rg, sub) in writes:
            st = self._region(rg)
            if sub == ALL:
                for k, (w, rd) in st.items():
                    add_dep(w)
                    for d in rd:
                        add_dep(d)
            else:
                for k in (sub, ALL, "__pending__"):
                    w, rd = st.get(k, (None, []))
                    add_dep(w)
                    for d in rd:
                        add_dep(d)

        needed = []
        seen = set()
        for d in deps:
            if id(d) in seen:
                continue
            seen.add(id(d))
            same = d.stream is s
            if same and not d.is_dma and not is_dma and s.name == "ten":
                continue
            if d.is_dma:
                k = d.stream.dkey(d.idx % R_DMA)
                done = s.clock.get(k, 0)
                if done >= _dma_cnt(d.idx, d.idx % R_DMA):
                    continue
            else:
                k = d.stream.ckey()
                if s.clock.get(k, -1) >= d.idx:
                    continue
            needed.append(d)
        best = {}
        for d in needed:
            k = d.stream.dkey(d.idx % R_DMA) if d.is_dma else d.stream.ckey()
            if k not in best or best[k].idx < d.idx:
                best[k] = d
        op.waits = list(best.values())

        def learn(clock, d):
            if d.is_dma:
                t = d.stream
                j0 = d.idx % R_DMA
                k = t.dkey(j0)
                v = _dma_cnt(d.idx, j0)
                if clock.get(k, 0) < v:
                    clock[k] = v
                m = d.idx - R_DMA
                for j in range(R_DMA):
                    vv = _dma_cnt(m, j)
                    kk = t.dkey(j)
                    if vv and clock.get(kk, 0) < vv:
                        clock[kk] = vv
            else:
                k = d.stream.ckey()
                if clock.get(k, -1) < d.idx:
                    clock[k] = d.idx
            if d.vclock:
                for kk, vv in d.vclock.items():
                    if clock.get(kk, -1 if len(kk) == 2 else 0) < vv:
                        clock[kk] = vv

        for d in op.waits:
            d.inc_needed = True
            learn(s.clock, d)

        if is_dma and idx >= R_DMA:
            m = idx - R_DMA
            for j in range(R_DMA):
                vv = _dma_cnt(m, j)
                kk = s.dkey(j)
                if vv and s.clock.get(kk, 0) < vv:
                    s.clock[kk] = vv

        vclock = dict(s.clock)
        if is_dma:
            j0 = idx % R_DMA
            k = s.dkey(j0)
            vclock[k] = max(vclock.get(k, 0), _dma_cnt(idx, j0))
            m = idx - R_DMA
            for j in range(R_DMA):
                vv = _dma_cnt(m, j)
                kk = s.dkey(j)
                if vv and vclock.get(kk, 0) < vv:
                    vclock[kk] = vv
        else:
            k = s.ckey()
            vclock[k] = max(vclock.get(k, -1), idx)
        op.vclock = vclock

        for (rg, sub) in reads:
            st = self._region(rg)
            w, rd = st.setdefault(sub, (None, []))
            rd.append(op)
        for (rg, sub) in writes:
            st = self._region(rg)
            if sub == ALL:
                self.state[rg] = {ALL: (op, [])}
            else:
                st[sub] = (op, [])

        s.ops.append(op)
        if is_dma:
            s.n_dma += 1
        else:
            s.n_comp += 1
        self.n_ops += 1
        return op

    def dma(self, stream, out, in_, reads=(), writes=(), tag="", **kw):
        def call(eng):
            return eng.dma_start(out=out, in_=in_, **kw)
        return self.op(stream, call, reads=reads, writes=writes, is_dma=True, tag=tag)

    def emit(self):
        nc = self.nc
        sems = {}
        for name in ENGINES:
            s = self.streams[name]
            if not s.ops:
                continue
            if s.n_comp:
                sems[s.ckey()] = nc.alloc_semaphore(name=f"s_{name}_c")
            for j in range(R_DMA):
                if s.n_dma > j:
                    sems[s.dkey(j)] = nc.alloc_semaphore(name=f"s_{name}_d{j}")

        for s in self.streams.values():
            cc = 0
            for op in s.ops:
                if op.is_dma:
                    op.sem_val = 16 * _dma_cnt(op.idx, op.idx % R_DMA)
                elif op.inc_needed:
                    cc += 1
                    op.sem_val = cc
            s.final_c = cc

        def replay(name):
            s = self.streams[name]

            def body(eng):
                for op in s.ops:
                    for d in op.waits:
                        if d.is_dma:
                            k = d.stream.dkey(d.idx % R_DMA)
                        else:
                            k = d.stream.ckey()
                        assert d.sem_val is not None, f"dep without inc: {d.tag}"
                        eng.wait_ge(sems[k], d.sem_val)
                    if op.is_dma and op.idx >= R_DMA:
                        j0 = op.idx % R_DMA
                        eng.wait_ge(sems[s.dkey(j0)], 16 * _dma_cnt(op.idx - R_DMA, j0))
                    last = None
                    for c in op.calls:
                        last = c(eng)
                    if op.is_dma:
                        last.then_inc(sems[s.dkey(op.idx % R_DMA)], 16)
                    elif op.inc_needed:
                        last.then_inc(sems[s.ckey()], 1)
                for j in range(R_DMA):
                    if s.n_dma > j:
                        eng.wait_ge(sems[s.dkey(j)], 16 * _dma_cnt(s.n_dma - 1, j))
            return body

        with nc.Block() as blk:
            for name in ENGINES:
                if self.streams[name].ops:
                    getattr(blk, ENG_ATTR[name])(replay(name))

    def stats(self):
        out = {}
        for n, s in self.streams.items():
            nw = sum(len(op.waits) for op in s.ops)
            ni = sum(1 for op in s.ops if op.inc_needed or op.is_dma)
            ninst = sum(len(op.calls) for op in s.ops)
            out[n] = dict(ops=len(s.ops), instrs=ninst, waits=nw, incs=ni)
        return out




"""Phase A program: L0_attn (MLA causal attention, T=2048) + L0_ffn (6-layer inner stack).

Per core: NSEQ sequences (batch-shard). Feature-major [128 d-chunk, tokens].
x1 = x + MLA(LN(x));  x2 = x1 + Stack6(LN(x1)).
Inner layer i: y = attn_i(shift(y)) + y ; y = ffn_i(shift(y)) + y.

One fp32 stream tensor `xy` [128, 8*(T+1)] holds x (during L0_attn) then y (the
stack stream); x1 is spilled to HBM and re-added at the end.  Data col for
chunk c, token t: c*(T+1)+1+t; col c*(T+1) is a zero pad used by shift_tokens.
"""
P = 128
T = 2048
STAGE = 9
NIL = 6
W = 256
NW = T // W
NWP = NW // 2
NT = T // 512
NQT = T // 128
LN_EPS = 1e-5
DI2 = 2816
KI = DI2 // 128


def build_phase_a(NSEQ=4, do_l0attn=True, do_l0ffn=True):
    TY = T + 1
    nc = bass.Bass("TRN2", target_bir_lowering=False)

    xin = nc.dram_tensor("xin", [NSEQ, 8, P, T], F32, kind="ExternalInput").ap()
    wkvd = nc.dram_tensor("wkvd", [1024, P], F32, kind="ExternalInput").ap()
    wkvu = nc.dram_tensor("wkvu", [P, 2048], F32, kind="ExternalInput").ap()
    wq = nc.dram_tensor("wq", [1024, 1024], F32, kind="ExternalInput").ap()
    wout = nc.dram_tensor("wout", [1024, 1024], F32, kind="ExternalInput").ap()
    Wqkv6 = nc.dram_tensor("Wqkv6", [NIL, 1024, 3072], F32, kind="ExternalInput").ap()
    Wo6 = nc.dram_tensor("Wo6", [NIL, 1024, 1024], F32, kind="ExternalInput").ap()
    W1p = nc.dram_tensor("W1p", [NIL, 1024, 2 * DI2], F32, kind="ExternalInput").ap()
    W2p = nc.dram_tensor("W2p", [NIL, DI2, 1024], F32, kind="ExternalInput").ap()
    b1cd = nc.dram_tensor("b1cd", [NIL, P, 2 * KI], F32, kind="ExternalInput").ap()
    b2cd = nc.dram_tensor("b2cd", [NIL, P, 8], F32, kind="ExternalInput").ap()
    lngd = nc.dram_tensor("lngd", [NIL, P, 8], F32, kind="ExternalInput").ap()
    lnbd = nc.dram_tensor("lnbd", [NIL, P, 8], F32, kind="ExternalInput").ap()
    qsd = nc.dram_tensor("qsd", [P, 4 * NIL], F32, kind="ExternalInput").ap()
    TABS = nc.dram_tensor("TABS", [4, P, T], F32, kind="ExternalInput").ap()
    RT2d = nc.dram_tensor("RT2d", [P, P], F32, kind="ExternalInput").ap()
    identfd = nc.dram_tensor("identfd", [P, P], F32, kind="ExternalInput").ap()
    cmaskd = nc.dram_tensor("cmaskd", [P, P], F32, kind="ExternalInput").ap()
    wmaskd = nc.dram_tensor("wmaskd", [P, 2 * 512 + 2 * W], F32, kind="ExternalInput").ap()
    sel2d = nc.dram_tensor("sel2d", [2, 3 * P], F32, kind="ExternalInput").ap()
    xout = nc.dram_tensor("xout", [NSEQ, 8, P, T], F32, kind="ExternalOutput").ap()

    x1d = nc.dram_tensor("x1d", [8, P, T], F32, kind="Internal").ap()
    Wqkvb = nc.dram_tensor("Wqkvb", [NIL, 1024, 3072], BF16, kind="Internal").ap()
    Wob = nc.dram_tensor("Wob", [NIL, 1024, 1024], BF16, kind="Internal").ap()
    W1b = nc.dram_tensor("W1b", [NIL, 1024, 2 * DI2], BF16, kind="Internal").ap()
    W2b = nc.dram_tensor("W2b", [NIL, DI2, 1024], BF16, kind="Internal").ap()

    xy = nc.alloc_sbuf_tensor("xy", [P, 8 * TY], F32)
    arena = nc.alloc_sbuf_tensor("arena", [P, 32 * 1024], BF16)
    aout = nc.alloc_sbuf_tensor("aout", [P, max(2 * T, 8 * 512)], BF16)
    cbuf = nc.alloc_sbuf_tensor("cbuf", [P, 4 * 512], F32)
    wb = [nc.alloc_sbuf_tensor(f"wb{i}", [P, 1024], BF16) for i in range(6)]
    wbig = nc.alloc_sbuf_tensor("wbig", [P, KI * P], BF16)
    Pbf = [nc.alloc_sbuf_tensor(f"Pbf{i}", [P, 512], BF16) for i in range(2)]
    Ssb = [nc.alloc_sbuf_tensor(f"Ssb{i}", [P, 512], F32) for i in range(2)]
    PTb = nc.alloc_sbuf_tensor("PTb", [P, 16 * P], BF16)
    vTb = nc.alloc_sbuf_tensor("vTb", [P, 16 * 64], BF16)
    Ysb = nc.alloc_sbuf_tensor("Ysb", [P, 8 * 64], BF16)
    sq = [nc.alloc_sbuf_tensor(f"sq{i}", [P, 512], F32) for i in range(2)]
    strow = nc.alloc_sbuf_tensor("strow", [1, 4 * 512], F32)
    rs = nc.alloc_sbuf_tensor("rs", [P, 32], F32)
    stq = nc.alloc_sbuf_tensor("stq", [P, 16], F32)
    knq = nc.alloc_sbuf_tensor("knq", [4, 512], F32)
    qscol = nc.alloc_sbuf_tensor("qscol", [P, 2 * NQT * 8], F32)
    qsw = nc.alloc_sbuf_tensor("qsw", [P, 2 * 8 * 4 * 2], F32)
    lnw = nc.alloc_sbuf_tensor("lnw", [P, 2 * KI + 24], F32)
    qsc = nc.alloc_sbuf_tensor("qsc", [P, 4 * NIL], F32)
    RT2 = nc.alloc_sbuf_tensor("RT2", [P, P], BF16)
    identb = nc.alloc_sbuf_tensor("identb", [P, P], BF16)
    identf_sb = nc.alloc_sbuf_tensor("identf_sb", [P, P], F32)
    cmask = nc.alloc_sbuf_tensor("cmask", [P, P], F32)
    wmask = nc.alloc_sbuf_tensor("wmask", [P, 2 * 512 + 2 * W], BF16)
    sel2 = nc.alloc_sbuf_tensor("sel2", [2, 3 * P], F32)
    onesk = nc.alloc_sbuf_tensor("onesk", [P, 1], F32)
    onescol = nc.alloc_sbuf_tensor("onescol", [1, P], F32)
    onespair = nc.alloc_sbuf_tensor("onespair", [P, 2], F32)
    epsc = nc.alloc_sbuf_tensor("epsc", [P, 1], F32)
    t1f = nc.alloc_sbuf_tensor("t1f", [P, 512], F32)
    t2f = nc.alloc_sbuf_tensor("t2f", [P, 512], F32)
    qbb = [nc.alloc_sbuf_tensor(f"qbb{i}", [P, 512], BF16) for i in range(2)]
    latb = nc.alloc_sbuf_tensor("latb", [P, T], BF16)

    pbk = [nc.alloc_psum_tensor(f"pbk{i}", [P, 512], F32) for i in range(8)]
    pT = [pbk[6][:].bitcast(BF16), pbk[7][:].bitcast(BF16)]

    print("A sbuf bytes remaining:", nc.sbuf_bytes_remaining)

    pr = Prog(nc)

    pr.op("dve", lambda e: e.memset(onesk[:], 1.0), writes=["onesk"])
    pr.op("dve", lambda e: e.memset(onescol[:], 1.0), writes=["onescol"])
    pr.op("dve", lambda e: e.memset(onespair[:], 0.0), writes=["onespair"])
    pr.op("dve", lambda e: e.memset(onespair[0:64, 0:1], 1.0), writes=["onespair"])
    pr.op("dve", lambda e: e.memset(onespair[64:128, 1:2], 1.0), writes=["onespair"])
    pr.op("dve", lambda e: e.memset(epsc[:], LN_EPS), writes=["epsc"])
    pr.dma("pool", RT2[:], RT2d, writes=["RT2"])
    pr.dma("pool", identb[:], identfd, writes=["identb"])
    pr.dma("sp", identf_sb[:], identfd, writes=["identf_sb"])
    pr.dma("sp", cmask[:], cmaskd, writes=["cmask"])
    pr.dma("pool", wmask[:], wmaskd, writes=["wmask"])
    pr.dma("sp", sel2[:], sel2d, writes=["sel2"])
    pr.dma("sp", qsc[:], qsd, writes=["qsc"])

    if do_l0ffn:
        pr.clear_region("arena")
        cbv = [arena[:, 0:8192], arena[:, 8192:16384]]
        ci = [0]

        def precast(dst, src, rows, cols, nm, il):
            for r in range(rows // P):
                b = cbv[ci[0] % 2]
                pr.dma("pool", b[:, 0:cols], src[r * P:(r + 1) * P, :],
                       writes=[("arena", ("cb", ci[0] % 2))], tag=f"pc{ci[0]}")
                pr.dma("sp", dst[r * P:(r + 1) * P, :], b[:, 0:cols],
                       reads=[("arena", ("cb", ci[0] % 2))],
                       writes=[("wdram", (nm, il, r))], tag=f"pc2{ci[0]}")
                ci[0] += 1
        for il in range(NIL):
            precast(Wqkvb[il], Wqkv6[il], 1024, 3072, "qkv", il)
            precast(Wob[il], Wo6[il], 1024, 1024, "wo", il)
            precast(W1b[il], W1p[il], 1024, 2 * DI2, "w1", il)
            precast(W2b[il], W2p[il], DI2, 1024, "w2", il)

    def stream_wbf(q, dst, dsrc, m, kch, tag="", key=None, rkeys=()):
        src = dsrc.rearrange("(k p) (m q) -> p m k q", p=P, q=P)[:, m, :, :]
        pr.dma(q, dst[:, 0:kch * P].rearrange("p (k q) -> p k q", q=P), src,
               reads=list(rkeys), writes=[key or (dst.name, ALL)], tag=tag)

    def ln_tile(src_reads, src_fn, dst_fn, dst_key, t0, tag, gamma=None, beta=None, dst_keys=None):
        dkf = dst_keys if dst_keys is not None else (lambda c, tt0, n: [(dst_key, c)])
        mms = []
        for c in range(8):
            def mk(c=c):
                return lambda e: e.matmul(pbk[5][0:1, 0:512], onesk[:, 0:1],
                                          src_fn(c, t0, 512), start=(c == 0), stop=(c == 7))
            mms.append(mk())
        pr.op("ten", mms, reads=list(src_reads) + ["onesk"], writes=[("psum", 5)], tag=f"{tag}.sm")
        for c in range(8):
            sqb = sq[c % 2]
            pr.op("act", (lambda c=c, sqb=sqb: lambda e: e.square(sqb[:], src_fn(c, t0, 512)))(),
                  reads=list(src_reads), writes=[("sq", c % 2)], tag=f"{tag}.q{c}")
            pr.op("ten", (lambda c=c, sqb=sqb: lambda e: e.matmul(
                pbk[4][0:1, 0:512], onesk[:, 0:1], sqb[:], start=(c == 0), stop=(c == 7)))(),
                reads=[("sq", c % 2), "onesk"], writes=[("psum", 4)], tag=f"{tag}.q2{c}")
        pr.op("dve", (lambda: lambda e: e.tensor_scalar_mul(
            strow[0:1, 0:512], pbk[5][0:1, 0:512], 1.0 / 1024.0))(),
            reads=[("psum", 5)], writes=[("strow", 0)], tag=f"{tag}.mu")
        pr.op("dve", (lambda: lambda e: e.tensor_scalar_mul(
            strow[0:1, 1024:1536], pbk[4][0:1, 0:512], 1.0 / 1024.0))(),
            reads=[("psum", 4)], writes=[("strow", 2)], tag=f"{tag}.ms")
        pr.op("act", (lambda: lambda e: e.square(strow[0:1, 1536:2048], strow[0:1, 0:512]))(),
              reads=[("strow", 0)], writes=[("strow", 3)], tag=f"{tag}.m2")
        pr.op("dve", (lambda: lambda e: e.tensor_sub(
            strow[0:1, 1024:1536], strow[0:1, 1024:1536], strow[0:1, 1536:2048]))(),
            reads=[("strow", 3), ("strow", 2)], writes=[("strow", 2)], tag=f"{tag}.v")
        pr.op("act", (lambda: lambda e: e.activation(
            strow[0:1, 512:1024], strow[0:1, 1024:1536], AF.Sqrt, bias=epsc[0:1, 0:1]))(),
            reads=[("strow", 2), "epsc"], writes=[("strow", 1)], tag=f"{tag}.sd")
        pr.op("dve", (lambda: lambda e: e.reciprocal(
            strow[0:1, 512:1024], strow[0:1, 512:1024]))(),
            reads=[("strow", 1)], writes=[("strow", 1)], tag=f"{tag}.rc")
        pr.op("ten", (lambda: lambda e: e.matmul(pbk[5][:, 0:512], onescol[0:1, :],
            strow[0:1, 0:512], start=True, stop=True))(),
            reads=[("strow", 0), "onescol"], writes=[("psum", 5)], tag=f"{tag}.b1")
        pr.op("ten", (lambda: lambda e: e.matmul(pbk[4][:, 0:512], onescol[0:1, :],
            strow[0:1, 512:1024], start=True, stop=True))(),
            reads=[("strow", 1), "onescol"], writes=[("psum", 4)], tag=f"{tag}.b2")
        for c in range(8):
            sqb = sq[c % 2]
            pr.op("dve", (lambda c=c, sqb=sqb: lambda e: e.tensor_sub(
                sqb[:], src_fn(c, t0, 512), pbk[5][:, 0:512]))(),
                reads=list(src_reads) + [("psum", 5)], writes=[("sq", c % 2)], tag=f"{tag}.c{c}")
            if gamma is None:
                pr.op("dve", (lambda c=c, sqb=sqb: lambda e: e.tensor_mul(
                    dst_fn(c, t0, 512), sqb[:], pbk[4][:, 0:512]))(),
                    reads=[("sq", c % 2), ("psum", 4)], writes=dkf(c, t0, 512), tag=f"{tag}.n{c}")
            else:
                pr.op("dve", (lambda c=c, sqb=sqb: lambda e: e.tensor_mul(
                    sqb[:], sqb[:], pbk[4][:, 0:512]))(),
                    reads=[("sq", c % 2), ("psum", 4)], writes=[("sq", c % 2)], tag=f"{tag}.g{c}")
                pr.op("dve", (lambda c=c, sqb=sqb: lambda e: e.tensor_scalar(
                    dst_fn(c, t0, 512), sqb[:], gamma(c), beta(c), ALU.mult, ALU.add))(),
                    reads=[("sq", c % 2), "lnw"], writes=dkf(c, t0, 512), tag=f"{tag}.n{c}")

    XC = lambda c, t: c * TY + 1 + t
    QB = 8 * T
    wdk = lambda nm, il, n: [("wdram", (nm, il, r)) for r in range(n)]

    def ab(col, width):
        return [("arena", i) for i in range(col // 512, (col + width + 511) // 512)]

    def aob(col, width):
        return [("aout", i) for i in range(col // 512, (col + width + 511) // 512)]

    def vtb(col, width):
        return [("vTb", i) for i in range(col // 64, (col + width + 63) // 64)]

    def l0attn(s_i):
        for tt in range(NT):
            ln_tile([("xy", c) for c in range(8)],
                    lambda c, t0, n: xy[:, XC(c, t0):XC(c, t0) + n],
                    lambda c, t0, n: arena[:, QB + c * T + t0:QB + c * T + t0 + n],
                    None, tt * 512, tag=f"lnA{s_i}.{tt}",
                    dst_keys=lambda c, t0, n: ab(QB + c * T + t0, n))
        stream_wbf("pool", wb[0], wkvd, 0, 8, tag=f"wkvd{s_i}")
        for tt in range(NT):
            t0 = tt * 512
            mms = []
            for k in range(8):
                def mk(k=k, t0=t0, tt=tt):
                    return lambda e: e.matmul(pbk[tt % 2][:], wb[0][:, k * P:(k + 1) * P],
                                              arena[:, QB + k * T + t0:QB + k * T + t0 + 512],
                                              start=(k == 0), stop=(k == 7))
                mms.append(mk())
            pr.op("ten", mms, reads=[(wb[0].name, ALL)] + sum([ab(QB + c * T + t0, 512) for c in range(8)], []),
                  writes=[("psum", tt % 2)], tag=f"lat{s_i}.{tt}")
            pr.op("dve", (lambda t0=t0, tt=tt: lambda e: e.tensor_copy(
                latb[:, t0:t0 + 512], pbk[tt % 2][:]))(),
                reads=[("psum", tt % 2)], writes=["latb"], tag=f"late{s_i}.{tt}")
        for m in range(8):
            w = wb[1 + m % 2]
            stream_wbf("pool", w, wq, m, 8, tag=f"wq{s_i}.{m}")
            for tt in range(NT):
                t0 = tt * 512
                mms = []
                for k in range(8):
                    def mk(k=k, t0=t0, w=w, tt=tt):
                        return lambda e: e.matmul(pbk[2 + tt % 2][:], w[:, k * P:(k + 1) * P],
                                                  arena[:, QB + k * T + t0:QB + k * T + t0 + 512],
                                                  start=(k == 0), stop=(k == 7))
                    mms.append(mk())
                pr.op("ten", mms, reads=[(w.name, ALL)] + sum([ab(QB + c * T + t0, 512) for c in range(8)], []),
                      writes=[("psum", 2 + tt % 2)], tag=f"q{s_i}.{m}.{tt}")
                pr.op("dve", (lambda m=m, t0=t0, tt=tt: lambda e: e.tensor_copy(
                    arena[:, m * T + t0:m * T + t0 + 512], pbk[2 + tt % 2][:]))(),
                    reads=[("psum", 2 + tt % 2)], writes=ab(m * T + t0, 512), tag=f"qe{s_i}.{m}.{tt}")
        for c in range(8):
            for tt in range(NT):
                t0 = tt * 512
                sqb = sq[tt % 2]
                pr.op("act", (lambda c=c, t0=t0, sqb=sqb: lambda e: e.square(
                    sqb[:], arena[:, c * T + t0:c * T + t0 + 512]))(),
                    reads=ab(c * T + t0, 512), writes=[("sq", tt % 2)], tag=f"qn{s_i}.{c}.{tt}")
                pr.op("ten", (lambda sqb=sqb: lambda e: e.matmul(
                    pbk[5][0:2, 0:512], onespair[:], sqb[:], start=True, stop=True))(),
                    reads=[("sq", tt % 2), "onespair"], writes=[("psum", 5)], tag=f"qn2{s_i}.{c}.{tt}")
                pr.op("act", (lambda: lambda e: e.activation(
                    knq[0:2, 0:512], pbk[5][0:2, 0:512], AF.Sqrt))(),
                    reads=[("psum", 5)], writes=["knq"], tag=f"qn3{s_i}.{c}.{tt}")
                for q4 in range(4):
                    qt = tt * 4 + q4
                    pr.op("ten", (lambda q4=q4: lambda e: e.transpose(
                        pbk[5][0:P, 8:10], knq[0:2, q4 * 128:q4 * 128 + 128], identf_sb[0:2, 0:2]))(),
                        reads=["knq", "identf_sb"], writes=[("psum", 5)], tag=f"qt{s_i}.{c}.{qt}")
                    pr.op("dve", (lambda c=c, qt=qt: lambda e: e.tensor_copy(
                        qscol[:, (c * NQT + qt) * 2:(c * NQT + qt) * 2 + 2], pbk[5][0:P, 8:10]))(),
                        reads=[("psum", 5)], writes=[("qscol", (c, qt))], tag=f"qte{s_i}.{c}.{qt}")

        for half in range(2):
            for part, cofs in ((0, half * 512), (1, 1024 + half * 512)):
                for m4 in range(4):
                    mm = cofs // P + m4
                    wcur = wb[3 + m4 % 2]
                    src = wkvu.rearrange("p (m q) -> p m q", q=P)[:, mm, :]
                    pr.dma("pool", wcur[:, 0:P], src, writes=[(wcur.name, ALL)],
                           tag=f"wkvu{s_i}.{half}.{part}.{m4}")
                    for tt in range(NT):
                        t0 = tt * 512
                        pr.op("ten", (lambda t0=t0, wcur=wcur, tt=tt: lambda e: e.matmul(
                            pbk[tt % 2][:], wcur[:, 0:P], latb[:, t0:t0 + 512],
                            start=True, stop=True))(),
                            reads=[(wcur.name, ALL), "latb"], writes=[("psum", tt % 2)],
                            tag=f"kv{s_i}.{half}.{part}.{m4}.{tt}")
                        dstc = QB + (part * 4 + m4) * T + t0
                        pr.op("dve", (lambda dstc=dstc, tt=tt: lambda e: e.tensor_copy(
                            arena[:, dstc:dstc + 512], pbk[tt % 2][:]))(),
                            reads=[("psum", tt % 2)], writes=ab(dstc, 512),
                            tag=f"kve{s_i}.{half}.{part}.{m4}.{tt}")
            for c4 in range(4):
                for tt in range(NT):
                    t0 = tt * 512
                    sqb = sq[tt % 2]
                    pr.op("act", (lambda c4=c4, t0=t0, sqb=sqb: lambda e: e.square(
                        sqb[:], arena[:, QB + c4 * T + t0:QB + c4 * T + t0 + 512]))(),
                        reads=ab(QB + c4 * T + t0, 512), writes=[("sq", tt % 2)], tag=f"kn{s_i}.{half}.{c4}.{tt}")
                    pr.op("ten", (lambda sqb=sqb: lambda e: e.matmul(
                        pbk[5][0:2, 0:512], onespair[:], sqb[:], start=True, stop=True))(),
                        reads=[("sq", tt % 2), "onespair"], writes=[("psum", 5)],
                        tag=f"kn2{s_i}.{half}.{c4}.{tt}")
                    pr.op("dve", (lambda c4=c4, tt=tt: lambda e: e.reduce_max(
                        knq[0:2, 8 * c4 + tt:8 * c4 + tt + 1], pbk[5][0:2, 0:512], axis=AX.X))(),
                        reads=[("psum", 5)], writes=["knq"],
                        tag=f"kn3{s_i}.{half}.{c4}.{tt}")
                pr.op("dve", (lambda c4=c4: lambda e: e.reduce_max(
                    knq[0:2, 8 * c4 + 4:8 * c4 + 5], knq[0:2, 8 * c4:8 * c4 + NT], axis=AX.X))(),
                    reads=["knq"],
                    writes=["knq"], tag=f"kn4{s_i}.{half}.{c4}")
                pr.op("act", (lambda c4=c4: lambda e: e.activation(
                    knq[0:2, 8 * c4 + 4:8 * c4 + 5], knq[0:2, 8 * c4 + 4:8 * c4 + 5], AF.Sqrt))(),
                    reads=["knq"], writes=["knq"],
                    tag=f"kn5{s_i}.{half}.{c4}")

            for quarter in range(2):
                for hh4 in range(4):
                    hh = quarter * 4 + hh4
                    h = half * 8 + hh
                    c4, hb = hh // 2, 64 * (hh % 2)
                    qch, qhb = h // 2, 64 * (h % 2)
                    aoch = qch % 2
                    pr.op("ten", (lambda hh=hh, c4=c4: lambda e: e.matmul(
                        pbk[5][0:P, 8:9], sel2[0:2, (hh % 2) * P:(hh % 2) * P + P],
                        knq[0:2, 8 * c4 + 4:8 * c4 + 5], start=True, stop=True))(),
                        reads=["knq", "sel2"], writes=[("psum", 5)], tag=f"mkb{s_i}.{h}")
                    pr.op("dve", (lambda: lambda e: e.tensor_scalar_mul(
                        stq[:, 0:1], pbk[5][0:P, 8:9], -0.125))(),
                        reads=[("psum", 5)], writes=[("stq", "mkb")], tag=f"mkbe{s_i}.{h}")
                    for kb in range(NQT):
                        tb = pT[kb % 2]
                        vcol = QB + (4 + c4) * T + kb * 128
                        pr.op("ten", (lambda kb=kb, tb=tb, vcol=vcol: lambda e: e.transpose(
                            tb[0:P, (kb % 8) * 128:(kb % 8) * 128 + 128],
                            arena[0:P, vcol:vcol + 128], identb[:]))(),
                            reads=ab(QB + (4 + c4) * T + kb * 128, 128) + ["identb"], writes=[("psum", 6 + kb % 2)],
                            tag=f"vt{s_i}.{h}.{kb}")
                        pr.op("dve", (lambda kb=kb, tb=tb, hb=hb: lambda e: e.tensor_copy(
                            vTb[:, kb * 64:kb * 64 + 64],
                            tb[0:P, (kb % 8) * 128 + hb:(kb % 8) * 128 + hb + 64]))(),
                            reads=[("psum", 6 + kb % 2)], writes=vtb(kb * 64, 64), tag=f"vte{s_i}.{h}.{kb}")
                    for qt in range(NQT):
                        kvlen = (qt + 1) * 128
                        nkb = (kvlen + 511) // 512
                        par = qt % 2
                        pr.op("dve", (lambda qch=qch, qt=qt, h=h, par=par: lambda e: e.tensor_mul(
                            stq[:, 1 + par:2 + par],
                            qscol[:, (qch * NQT + qt) * 2 + (h % 2):(qch * NQT + qt) * 2 + (h % 2) + 1],
                            stq[:, 0:1]))(),
                            reads=[("qscol", (qch, qt)), ("stq", "mkb")], writes=[("stq", ("b", par))],
                            tag=f"bc{s_i}.{h}.{qt}")
                        for kb in range(nkb):
                            wlen = min(512, kvlen - kb * 512)
                            bnk = kb % 4
                            pr.op("ten", (lambda qch=qch, qhb=qhb, qt=qt, kb=kb, c4=c4, hb=hb, wlen=wlen, bnk=bnk: lambda e: e.matmul(
                                pbk[bnk][:, 0:wlen],
                                arena[qhb:qhb + 64, qch * T + qt * 128:qch * T + qt * 128 + 128],
                                arena[hb:hb + 64, QB + c4 * T + kb * 512:QB + c4 * T + kb * 512 + wlen],
                                start=True, stop=True))(),
                                reads=ab(qch * T + qt * 128, 128) + ab(QB + c4 * T + kb * 512, wlen),
                                writes=[("psum", bnk)],
                                tag=f"S{s_i}.{h}.{qt}.{kb}")
                            if kb == nkb - 1:
                                pr.op("dve", (lambda bnk=bnk, wlen=wlen: lambda e: e.tensor_add(
                                    pbk[bnk][:, wlen - 128:wlen], pbk[bnk][:, wlen - 128:wlen], cmask[:]))(),
                                    reads=[("psum", bnk), "cmask"], writes=[("psum", bnk)],
                                    tag=f"msk{s_i}.{h}.{qt}")
                            pr.op("act", (lambda bnk=bnk, wlen=wlen, kb=kb, par=par: lambda e: e.activation(
                                Pbf[par][:, 0:wlen], pbk[bnk][:, 0:wlen], AF.Exp,
                                bias=stq[:, 1 + par:2 + par], scale=0.125,
                                accum_out=rs[:, kb:kb + 1]))(),
                                reads=[("psum", bnk), ("stq", ("b", par))],
                                writes=[("Pbf", par), ("rs", kb)], tag=f"ex{s_i}.{h}.{qt}.{kb}")
                            for b2 in range(0, wlen, 128):
                                gkb = kb * 4 + b2 // 128
                                tb = pT[gkb % 2]
                                pr.op("ten", (lambda par=par, b2=b2, tb=tb, gkb=gkb: lambda e: e.transpose(
                                    tb[0:P, (gkb % 8) * 128:(gkb % 8) * 128 + 128],
                                    Pbf[par][:, b2:b2 + 128], identb[:]))(),
                                    reads=[("Pbf", par), "identb"], writes=[("psum", 6 + gkb % 2)],
                                    tag=f"pt{s_i}.{h}.{qt}.{gkb}")
                                pr.op("act" if gkb % 2 else "dve",
                                      (lambda gkb=gkb, tb=tb, par=par:
                                       (lambda e: e.copy(PTb[:, gkb * 128:gkb * 128 + 128],
                                                         tb[0:P, (gkb % 8) * 128:(gkb % 8) * 128 + 128]))
                                       if gkb % 2 else
                                       (lambda e: e.tensor_copy(PTb[:, gkb * 128:gkb * 128 + 128],
                                                                tb[0:P, (gkb % 8) * 128:(gkb % 8) * 128 + 128])))(),
                                      reads=[("psum", 6 + gkb % 2)], writes=[("PTb", gkb)],
                                      tag=f"pte{s_i}.{h}.{qt}.{gkb}")
                        ngkb = kvlen // 128
                        mms = []
                        for gkb in range(ngkb):
                            def mkpv(gkb=gkb, ngkb=ngkb):
                                return lambda e: e.matmul(
                                    pbk[4][:, 0:64], PTb[:, gkb * 128:gkb * 128 + 128],
                                    vTb[:, gkb * 64:gkb * 64 + 64],
                                    start=(gkb == 0), stop=(gkb == ngkb - 1))
                            mms.append(mkpv())
                        pr.op("ten", mms,
                              reads=[("PTb", g) for g in range(ngkb)] + vtb(0, ngkb * 64),
                              writes=[("psum", 4)], tag=f"pv{s_i}.{h}.{qt}")
                        pr.op("dve", (lambda nkb=nkb: lambda e: e.reduce_sum(
                            rs[:, 8:9], rs[:, 0:nkb], axis=AX.X))(),
                            reads=[("rs", k2) for k2 in range(nkb)], writes=[("rs", "tot")],
                            tag=f"rt{s_i}.{h}.{qt}")
                        pr.op("dve", (lambda: lambda e: e.reciprocal(rs[:, 9:10], rs[:, 8:9]))(),
                              reads=[("rs", "tot")], writes=[("rs", "rc")], tag=f"rr{s_i}.{h}.{qt}")
                        ysl = qt % 8
                        pr.op("dve", (lambda ysl=ysl: lambda e: e.tensor_scalar_mul(
                            Ysb[:, ysl * 64:ysl * 64 + 64], pbk[4][:, 0:64], rs[:, 9:10]))(),
                            reads=[("psum", 4), ("rs", "rc")], writes=[("Ysb", ysl)],
                            tag=f"ye{s_i}.{h}.{qt}")
                        tb = pT[qt % 2]
                        pr.op("ten", (lambda ysl=ysl, tb=tb: lambda e: e.transpose(
                            tb[0:64, ysl * 128:ysl * 128 + 128],
                            Ysb[:, ysl * 64:ysl * 64 + 64], identb[:]))(),
                            reads=[("Ysb", ysl), "identb"], writes=[("psum", 6 + qt % 2)],
                            tag=f"yt{s_i}.{h}.{qt}")
                        pr.op("act", (lambda qt=qt, tb=tb, ysl=ysl, aoch=aoch, qhb=qhb: lambda e: e.copy(
                            aout[qhb:qhb + 64, aoch * T + qt * 128:aoch * T + qt * 128 + 128],
                            tb[0:64, ysl * 128:ysl * 128 + 128]))(),
                            reads=[("psum", 6 + qt % 2)], writes=aob(aoch * T + qt * 128, 128),
                            tag=f"yte{s_i}.{h}.{qt}")
                kbase = half * 4 + quarter * 2
                for m in range(8):
                    w = wb[1 + m % 2]
                    src = wout.rearrange("(k p) (m q) -> p m k q", p=P, q=P)[:, m, kbase:kbase + 2, :]
                    pr.dma("pool", w[:, 0:2 * P].rearrange("p (k q) -> p k q", q=P), src,
                           writes=[(w.name, ALL)], tag=f"wo{s_i}.{half}.{quarter}.{m}")
                    for tt in range(NT):
                        t0 = tt * 512
                        mms = []
                        for k2 in range(2):
                            def mk(k2=k2, t0=t0, w=w, tt=tt):
                                return lambda e: e.matmul(pbk[tt % 2][:], w[:, k2 * P:(k2 + 1) * P],
                                                          aout[:, k2 * T + t0:k2 * T + t0 + 512],
                                                          start=(k2 == 0), stop=(k2 == 1))
                            mms.append(mk())
                        pr.op("ten", mms, reads=[(w.name, ALL)] + aob(t0, 512) + aob(T + t0, 512),
                              writes=[("psum", tt % 2)], tag=f"wom{s_i}.{half}.{quarter}.{m}.{tt}")
                        pr.op("dve", (lambda m=m, t0=t0, tt=tt: lambda e: e.tensor_add(
                            xy[:, XC(m, t0):XC(m, t0) + 512], xy[:, XC(m, t0):XC(m, t0) + 512],
                            pbk[tt % 2][:]))(),
                            reads=[("psum", tt % 2), ("xy", m)], writes=[("xy", m)],
                            tag=f"woe{s_i}.{half}.{quarter}.{m}.{tt}")


    def l0ffn(s_i):
        for c in range(8):
            pr.dma("act", x1d[c], xy[:, XC(c, 0):XC(c, 0) + T], reads=[("xy", c)],
                   writes=[("x1d", c)], tag=f"spl{s_i}.{c}")
        for c in range(8):
            pr.op("dve", (lambda c=c: lambda e: e.memset(xy[:, c * TY:c * TY + 1], 0.0))(),
                  writes=[("xy", c)], tag=f"zp{s_i}.{c}")
        for tt in range(NT):
            ln_tile([("xy", c) for c in range(8)],
                    lambda c, t0, n: xy[:, XC(c, t0):XC(c, t0) + n],
                    lambda c, t0, n: xy[:, XC(c, t0):XC(c, t0) + n],
                    "xy", tt * 512, tag=f"lnY{s_i}.{tt}")

        QT_, KT_, VT_ = 0, 8 * 512, 16 * 512
        PKT = 24 * 512
        PVT = PKT + 8 * 256
        XB_ = 16 * 1024
        XW, AB = 0, 8 * 1024

        for il in range(NIL):
            pr.dma("sp", lnw[:, 0:2 * KI], b1cd[il], writes=[("lnw", "b1")], tag=f"lb1{s_i}.{il}")
            pr.dma("sp", lnw[:, 2 * KI:2 * KI + 8], b2cd[il], writes=[("lnw", "b2")], tag=f"lb2{s_i}.{il}")
            pr.dma("sp", lnw[:, 2 * KI + 8:2 * KI + 16], lngd[il], writes=[("lnw", "g")], tag=f"lg{s_i}.{il}")
            pr.dma("sp", lnw[:, 2 * KI + 16:2 * KI + 24], lnbd[il], writes=[("lnw", "b")], tag=f"lb{s_i}.{il}")

            for c in range(8):
                if STAGE < 2:
                    break
                sofs = XC(c, 0) - (1 if c >= 4 else 0)
                for tt in range(NT):
                    pr.op(["dve", "act"][c % 2],
                          (lambda c=c, sofs=sofs, tt=tt: lambda e: (
                              e.tensor_copy if c % 2 == 0 else e.copy)(
                              arena[:, XB_ + c * T + tt * 512:XB_ + c * T + tt * 512 + 512],
                              xy[:, sofs + tt * 512:sofs + tt * 512 + 512]))(),
                          reads=[("xy", c)], writes=ab(XB_ + c * T + tt * 512, 512),
                          tag=f"xb{s_i}.{il}.{c}.{tt}")

            for wp in range(NWP):
                t0 = wp * 512
                for ti in range(4):
                    pr.dma("act", cbuf[:, ti * 512:(ti + 1) * 512], TABS[ti, :, t0:t0 + 512],
                           writes=[("cbuf", ti)], tag=f"tb{s_i}.{il}.{wp}.{ti}")
                for m in range(24):
                    if STAGE < 3:
                        break
                    w = wb[m % 2]
                    stream_wbf("sp", w, Wqkvb[il], m, 8, tag=f"wqkv{s_i}.{il}.{wp}.{m}",
                               rkeys=wdk("qkv", il, 8))
                    mms = []
                    for k in range(8):
                        ofs = XB_ + k * T + t0
                        def mk(k=k, w=w, ofs=ofs, m=m):
                            return lambda e: e.matmul(pbk[m % 2][:], w[:, k * P:(k + 1) * P],
                                                      arena[:, ofs:ofs + 512], start=(k == 0), stop=(k == 7))
                        mms.append(mk())
                    pr.op("ten", mms, reads=[(w.name, ALL)] + sum(
                        [ab(XB_ + k * T + t0, 512) for k in range(8)], []),
                          writes=[("psum", m % 2)], tag=f"pj{s_i}.{il}.{wp}.{m}")
                    if m >= 16:
                        c = m - 16
                        pr.op("dve", (lambda c=c, m=m: lambda e: e.tensor_copy(
                            arena[:, VT_ + c * 512:VT_ + c * 512 + 512], pbk[m % 2][:]))(),
                            reads=[("psum", m % 2)], writes=ab(VT_ + c * 512, 512), tag=f"pv0{s_i}.{il}.{wp}.{m}")
                        continue
                    isq = m < 8
                    c = m if isq else m - 8
                    qb = qbb[m % 2]
                    pr.op("dve", (lambda qb=qb, m=m: lambda e: e.tensor_copy(qb[:], pbk[m % 2][:]))(),
                          reads=[("psum", m % 2)], writes=[(qb.name, ALL)], tag=f"qb{s_i}.{il}.{wp}.{m}")
                    pr.op("ten", (lambda qb=qb, m=m: lambda e: e.matmul(
                        pbk[2 + m % 2][:], RT2[:], qb[:], start=True, stop=True))(),
                        reads=[(qb.name, ALL), "RT2"], writes=[("psum", 2 + m % 2)],
                        tag=f"rot{s_i}.{il}.{wp}.{m}")
                    pr.op("act", (lambda qb=qb, m=m: lambda e: e.square(sq[m % 2][:], qb[:]))(),
                          reads=[(qb.name, ALL)], writes=[("sq", m % 2)], tag=f"n1{s_i}.{il}.{wp}.{m}")
                    pr.op("ten", (lambda m=m: lambda e: e.matmul(
                        pbk[5][0:2, 0:512], onespair[:], sq[m % 2][:], start=True, stop=True))(),
                        reads=[("sq", m % 2), "onespair"], writes=[("psum", 5)],
                        tag=f"n2{s_i}.{il}.{wp}.{m}")
                    pr.op("act", (lambda: lambda e: e.activation(
                        knq[0:2, 0:512], pbk[5][0:2, 0:512], AF.Sqrt))(),
                        reads=[("psum", 5)], writes=["knq"], tag=f"n3{s_i}.{il}.{wp}.{m}")
                    if isq:
                        pr.op("dve", (lambda: lambda e: e.tensor_scalar_max(
                            knq[0:2, 0:512], knq[0:2, 0:512], 1e-12))(),
                            reads=["knq"], writes=["knq"], tag=f"n4{s_i}.{il}.{wp}.{m}")
                        pr.op("dve", (lambda: lambda e: e.reciprocal(knq[0:2, 0:512], knq[0:2, 0:512]))(),
                              reads=["knq"], writes=["knq"], tag=f"n5{s_i}.{il}.{wp}.{m}")
                        pr.op("dve", (lambda: lambda e: e.tensor_scalar_mul(
                            knq[0:2, 0:512], knq[0:2, 0:512], 8.0))(),
                            reads=["knq"], writes=["knq"], tag=f"n6{s_i}.{il}.{wp}.{m}")
                        for q4 in range(4):
                            pr.op("ten", (lambda q4=q4: lambda e: e.transpose(
                                pbk[5][0:P, 8:10], knq[0:2, q4 * 128:q4 * 128 + 128], identf_sb[0:2, 0:2]))(),
                                reads=["knq", "identf_sb"], writes=[("psum", 5)],
                                tag=f"n8{s_i}.{il}.{wp}.{m}.{q4}")
                            pr.op("dve", (lambda c=c, q4=q4: lambda e: e.tensor_copy(
                                qsw[:, (c * 4 + q4) * 2:(c * 4 + q4) * 2 + 2], pbk[5][0:P, 8:10]))(),
                                reads=[("psum", 5)], writes=[("qsw", (c, q4, 0))],
                                tag=f"n9{s_i}.{il}.{wp}.{m}.{q4}")
                            pr.op("dve", (lambda c=c, q4=q4: lambda e: e.tensor_scalar_mul(
                                qsw[:, 64 + (c * 4 + q4) * 2:64 + (c * 4 + q4) * 2 + 2],
                                pbk[5][0:P, 8:10], -1.0))(),
                                reads=[("psum", 5)], writes=[("qsw", (c, q4, 1))],
                                tag=f"nB{s_i}.{il}.{wp}.{m}.{q4}")
                    else:
                        pr.op("dve", (lambda: lambda e: e.tensor_scalar_max(
                            knq[0:2, 0:512], knq[0:2, 0:512], 1e-12))(),
                            reads=["knq"], writes=["knq"], tag=f"n4{s_i}.{il}.{wp}.{m}")
                        pr.op("dve", (lambda: lambda e: e.reciprocal(knq[0:2, 0:512], knq[0:2, 0:512]))(),
                              reads=["knq"], writes=["knq"], tag=f"n5{s_i}.{il}.{wp}.{m}")
                        pr.op("ten", (lambda: lambda e: e.matmul(
                            pbk[4][:, 0:512], sel2[0:2, 2 * P:3 * P],
                            knq[0:2, 0:512], start=True, stop=True))(),
                            reads=["knq", "sel2"], writes=[("psum", 4)],
                            tag=f"n6{s_i}.{il}.{wp}.{m}")
                    qcol = qsc[:, (0 if isq else 2) * NIL + il:(0 if isq else 2) * NIL + il + 1]
                    pcol = qsc[:, (1 if isq else 3) * NIL + il:(1 if isq else 3) * NIL + il + 1]
                    cA = cbuf[:, (0 if isq else 2 * 512) + 0:0] if False else (
                        cbuf[:, 0:512] if isq else cbuf[:, 1024:1536])
                    cB = cbuf[:, 512:1024] if isq else cbuf[:, 1536:2048]
                    pr.op("dve", (lambda qb=qb, qcol=qcol, cA=cA: lambda e: e.scalar_tensor_tensor(
                        t1f[:], qb[:], qcol, cA, ALU.mult, ALU.mult))(),
                        reads=[(qb.name, ALL), "qsc", ("cbuf", 0 if isq else 2)], writes=["t1f"],
                        tag=f"r1{s_i}.{il}.{wp}.{m}")
                    pr.op("dve", (lambda m=m, pcol=pcol, cB=cB: lambda e: e.scalar_tensor_tensor(
                        t2f[:], pbk[2 + m % 2][:], pcol, cB, ALU.mult, ALU.mult))(),
                        reads=[("psum", 2 + m % 2), "qsc", ("cbuf", 1 if isq else 3)], writes=["t2f"],
                        tag=f"r2{s_i}.{il}.{wp}.{m}")
                    if isq:
                        pr.op("pool", (lambda c=c: lambda e: e.tensor_add(
                            arena[:, QT_ + c * 512:QT_ + c * 512 + 512], t1f[:], t2f[:]))(),
                            reads=["t1f", "t2f"], writes=ab(QT_ + c * 512, 512), tag=f"r3{s_i}.{il}.{wp}.{m}")
                    else:
                        pr.op("pool", (lambda: lambda e: e.tensor_add(t1f[:], t1f[:], t2f[:]))(),
                              reads=["t1f", "t2f"], writes=["t1f"], tag=f"r3{s_i}.{il}.{wp}.{m}")
                        pr.op("dve", (lambda c=c: lambda e: e.tensor_mul(
                            arena[:, KT_ + c * 512:KT_ + c * 512 + 512], t1f[:], pbk[4][:, 0:512]))(),
                            reads=["t1f", ("psum", 4)], writes=ab(KT_ + c * 512, 512), tag=f"r4{s_i}.{il}.{wp}.{m}")

                for wloc in range(2):
                    if STAGE < 4:
                        break
                    w_i = wp * 2 + wloc
                    cw0 = wloc * 256
                    for hh in range(16):
                        h = hh if hh < 8 else hh
                        ch, hb = h // 2, 64 * (h % 2)
                        for qt in range(2):
                            par = qt
                            q0 = cw0 + qt * 128
                            smms = []
                            if w_i > 0:
                                def mkS0(ch=ch, hb=hb, q0=q0, wloc=wloc, qt=qt):
                                    if wloc == 0:
                                        return lambda e: e.matmul(
                                            pbk[qt % 2][:, 0:256],
                                            arena[hb:hb + 64, QT_ + ch * 512 + q0:QT_ + ch * 512 + q0 + 128],
                                            arena[hb:hb + 64, PKT + ch * 256:PKT + ch * 256 + 256],
                                            start=True, stop=True)
                                    else:
                                        return lambda e: e.matmul(
                                            pbk[qt % 2][:, 0:256],
                                            arena[hb:hb + 64, QT_ + ch * 512 + q0:QT_ + ch * 512 + q0 + 128],
                                            arena[hb:hb + 64, KT_ + ch * 512:KT_ + ch * 512 + 256],
                                            start=True, stop=True)
                                smms.append(mkS0())
                            def mkS1(ch=ch, hb=hb, q0=q0, cw0=cw0, qt=qt, w_i=w_i):
                                ofs = 0 if w_i > 0 else 0
                                return lambda e: e.matmul(
                                    pbk[qt % 2][:, (256 if w_i > 0 else 0):(512 if w_i > 0 else 256)],
                                    arena[hb:hb + 64, QT_ + ch * 512 + q0:QT_ + ch * 512 + q0 + 128],
                                    arena[hb:hb + 64, KT_ + ch * 512 + cw0:KT_ + ch * 512 + cw0 + 256],
                                    start=True, stop=True)
                            smms.append(mkS1())
                            rd = ab(QT_ + ch * 512, 512) + ab(KT_ + ch * 512, 512) + (ab(PKT + ch * 256, 256) if (w_i > 0 and wloc == 0) else [])
                            pr.op("ten", smms, reads=rd, writes=[("psum", qt % 2)],
                                  tag=f"S{s_i}.{il}.{w_i}.{hh}.{qt}")
                            swid = 512 if w_i > 0 else 256
                            if w_i > 0:
                                mask = wmask[:, qt * 512:qt * 512 + 512]
                            elif qt == 0:
                                mask = wmask[:, 1024:1024 + 256]
                            else:
                                mask = wmask[:, 1280:1280 + 256]
                            pr.op("dve", (lambda par=par, swid=swid, mask=mask, qt=qt: lambda e: e.tensor_add(
                                Ssb[par][:, 0:swid], pbk[qt % 2][:, 0:swid], mask))(),
                                reads=[("psum", qt % 2), "wmask"], writes=[("Ssb", par)],
                                tag=f"ttr{s_i}.{il}.{w_i}.{hh}.{qt}")
                            pr.op("dve", (lambda par=par, swid=swid: lambda e: e.reduce_max(
                                rs[:, 10:11], Ssb[par][:, 0:swid], axis=AX.X))(),
                                reads=[("Ssb", par)], writes=[("rs", "mx")],
                                tag=f"ttm{s_i}.{il}.{w_i}.{hh}.{qt}")
                            q4 = cw0 // 128 + qt
                            pr.op("dve", (lambda ch=ch, q4=q4, h=h, par=par: lambda e: e.tensor_mul(
                                stq[:, 2 + par:3 + par],
                                qsw[:, 64 + (ch * 4 + q4) * 2 + (h % 2):64 + (ch * 4 + q4) * 2 + (h % 2) + 1],
                                rs[:, 10:11]))(),
                                reads=[("qsw", (ch, q4, 1)), ("rs", "mx")], writes=[("stq", ("fb", par))],
                                tag=f"fb{s_i}.{il}.{w_i}.{hh}.{qt}")
                            pr.op("act", (lambda par=par, swid=swid, ch=ch, q4=q4, h=h: lambda e: e.activation(
                                Pbf[par][:, 0:swid], Ssb[par][:, 0:swid], AF.Exp,
                                bias=stq[:, 2 + par:3 + par],
                                scale=qsw[:, (ch * 4 + q4) * 2 + (h % 2):(ch * 4 + q4) * 2 + (h % 2) + 1],
                                accum_out=rs[:, 12:13]))(),
                                reads=[("Ssb", par), ("stq", ("fb", par)), ("qsw", (ch, q4, 0))],
                                writes=[("Pbf", par), ("rs", "sum")],
                                tag=f"exw{s_i}.{il}.{w_i}.{hh}.{qt}")
                            nblk = swid // 128
                            for b2 in range(nblk):
                                tb = pT[b2 % 2]
                                pr.op("ten", (lambda par=par, b2=b2, tb=tb: lambda e: e.transpose(
                                    tb[0:P, (b2 % 8) * 128:(b2 % 8) * 128 + 128],
                                    Pbf[par][:, b2 * 128:b2 * 128 + 128], identb[:]))(),
                                    reads=[("Pbf", par), "identb"], writes=[("psum", 6 + b2 % 2)],
                                    tag=f"ptw{s_i}.{il}.{w_i}.{hh}.{qt}.{b2}")
                                pr.op("act" if b2 % 2 else "dve",
                                      (lambda b2=b2, tb=tb:
                                       (lambda e: e.copy(PTb[:, b2 * 128:b2 * 128 + 128],
                                                         tb[0:P, (b2 % 8) * 128:(b2 % 8) * 128 + 128]))
                                       if b2 % 2 else
                                       (lambda e: e.tensor_copy(PTb[:, b2 * 128:b2 * 128 + 128],
                                                                tb[0:P, (b2 % 8) * 128:(b2 % 8) * 128 + 128])))(),
                                      reads=[("psum", 6 + b2 % 2)], writes=[("PTb", b2)],
                                      tag=f"ptew{s_i}.{il}.{w_i}.{hh}.{qt}.{b2}")
                            if h % 2 == 0 and qt == 0:
                                for b2 in range(nblk):
                                    tb = pT[b2 % 2]
                                    if w_i > 0 and b2 < 2:
                                        if wloc == 0:
                                            vcol = PVT + ch * 256 + b2 * 128
                                        else:
                                            vcol = VT_ + ch * 512 + (cw0 - 256) + b2 * 128
                                    else:
                                        boff = (b2 - (2 if w_i > 0 else 0)) * 128 + cw0
                                        vcol = VT_ + ch * 512 + boff
                                    vsrc = arena[0:P, vcol:vcol + 128]
                                    pr.op("ten", (lambda tb=tb, b2=b2, vsrc=vsrc: lambda e: e.transpose(
                                        tb[0:P, (b2 % 8) * 128:(b2 % 8) * 128 + 128], vsrc, identb[:]))(),
                                        reads=ab(vcol, 128) + ["identb"], writes=[("psum", 6 + b2 % 2)],
                                        tag=f"vtw{s_i}.{il}.{w_i}.{ch}.{b2}")
                                    pr.op("dve", (lambda tb=tb, b2=b2, ch=ch: lambda e: e.tensor_copy(
                                        vTb[:, b2 * 128:b2 * 128 + 128],
                                        tb[0:P, (b2 % 8) * 128:(b2 % 8) * 128 + 128]))(),
                                        reads=[("psum", 6 + b2 % 2)], writes=vtb(b2 * 128, 128),
                                        tag=f"vtew{s_i}.{il}.{w_i}.{ch}.{b2}")
                            mms = []
                            for b2 in range(nblk):
                                def mkpv(b2=b2, nblk=nblk, hb=hb):
                                    return lambda e: e.matmul(
                                        pbk[4][:, 0:64], PTb[:, b2 * 128:b2 * 128 + 128],
                                        vTb[:, b2 * 128 + hb:b2 * 128 + hb + 64],
                                        start=(b2 == 0), stop=(b2 == nblk - 1))
                                mms.append(mkpv())
                            pr.op("ten", mms,
                                  reads=[("PTb", b2) for b2 in range(nblk)] + vtb(0, nblk * 128),
                                  writes=[("psum", 4)], tag=f"pvw{s_i}.{il}.{w_i}.{hh}.{qt}")
                            pr.op("dve", (lambda: lambda e: e.reciprocal(rs[:, 13:14], rs[:, 12:13]))(),
                                  reads=[("rs", "sum")], writes=[("rs", "rc")],
                                  tag=f"rrw{s_i}.{il}.{w_i}.{hh}.{qt}")
                            ysl = (hh * 2 + qt) % 8
                            pr.op("dve", (lambda ysl=ysl: lambda e: e.tensor_scalar_mul(
                                Ysb[:, ysl * 64:ysl * 64 + 64], pbk[4][:, 0:64], rs[:, 13:14]))(),
                                reads=[("psum", 4), ("rs", "rc")], writes=[("Ysb", ysl)],
                                tag=f"yew{s_i}.{il}.{w_i}.{hh}.{qt}")
                            tb = pT[qt % 2]
                            pr.op("ten", (lambda ysl=ysl, tb=tb: lambda e: e.transpose(
                                tb[0:64, ysl * 128:ysl * 128 + 128],
                                Ysb[:, ysl * 64:ysl * 64 + 64], identb[:]))(),
                                reads=[("Ysb", ysl), "identb"], writes=[("psum", 6 + qt % 2)],
                                tag=f"ytw{s_i}.{il}.{w_i}.{hh}.{qt}")
                            pr.op("act", (lambda tb=tb, ysl=ysl, ch=ch, hb=hb, q0=q0: lambda e: e.copy(
                                aout[hb:hb + 64, ch * 512 + q0:ch * 512 + q0 + 128],
                                tb[0:64, ysl * 128:ysl * 128 + 128]))(),
                                reads=[("psum", 6 + qt % 2)],
                                writes=aob(ch * 512 + q0, 128),
                                tag=f"ytew{s_i}.{il}.{w_i}.{hh}.{qt}")

                for c in range(8):
                    if STAGE < 5:
                        break
                    pr.op("pool", (lambda c=c: lambda e: e.tensor_copy(
                        arena[:, PKT + c * 256:PKT + c * 256 + 256],
                        arena[:, KT_ + c * 512 + 256:KT_ + c * 512 + 512]))(),
                        reads=ab(KT_ + c * 512 + 256, 256), writes=ab(PKT + c * 256, 256), tag=f"sk{s_i}.{il}.{wp}.{c}")
                    pr.op("pool", (lambda c=c: lambda e: e.tensor_copy(
                        arena[:, PVT + c * 256:PVT + c * 256 + 256],
                        arena[:, VT_ + c * 512 + 256:VT_ + c * 512 + 512]))(),
                        reads=ab(VT_ + c * 512 + 256, 256), writes=ab(PVT + c * 256, 256), tag=f"sv{s_i}.{il}.{wp}.{c}")
                for m in range(8):
                    if STAGE < 5:
                        break
                    w = wb[2 + m % 2]
                    stream_wbf("sp", w, Wob[il], m, 8, tag=f"wo{s_i}.{il}.{wp}.{m}",
                               rkeys=wdk("wo", il, 8))
                    mms = []
                    for k in range(8):
                        ac = k * 512
                        def mk(k=k, w=w, ac=ac, m=m):
                            return lambda e: e.matmul(pbk[m % 2][:], w[:, k * P:(k + 1) * P],
                                                      aout[:, ac:ac + 512], start=(k == 0), stop=(k == 7))
                        mms.append(mk())
                    pr.op("ten", mms, reads=[(w.name, ALL)] + sum([aob(k * 512, 512) for k in range(8)], []),
                          writes=[("psum", m % 2)], tag=f"wodm{s_i}.{il}.{wp}.{m}")
                    pr.op("dve", (lambda m=m, t0=t0: lambda e: e.tensor_add(
                        xy[:, XC(m, t0):XC(m, t0) + 512], xy[:, XC(m, t0):XC(m, t0) + 512],
                        pbk[m % 2][:]))(),
                        reads=[("psum", m % 2), ("xy", m)], writes=[("xy", m)],
                        tag=f"wode{s_i}.{il}.{wp}.{m}")
    

            NHF = max(1, T // 1024)
            HT = T // NHF
            for hf in reversed(range(NHF)):
                if STAGE < 6:
                    break
                h0 = hf * HT
                for t2 in range(HT // 512):
                    ln_tile([("xy", c) for c in range(8)],
                            lambda c, t0, n: xy[:, (XC(c, t0) if c < 4 else XC(c, t0) - 1):
                                                 (XC(c, t0) if c < 4 else XC(c, t0) - 1) + n],
                            (lambda h0=h0: lambda c, t0, n: arena[
                                :, XW + c * 1024 + (t0 - h0):XW + c * 1024 + (t0 - h0) + n])(),
                            None, h0 + t2 * 512, tag=f"lnF{s_i}.{il}.{hf}.{t2}",
                            gamma=lambda c: lnw[:, 2 * KI + 8 + c:2 * KI + 8 + c + 1],
                            beta=lambda c: lnw[:, 2 * KI + 16 + c:2 * KI + 16 + c + 1],
                            dst_keys=(lambda h0=h0: lambda c, tt0, n: ab(
                                XW + c * 1024 + (tt0 - h0), n))())
                for m in range(2 * KI):
                    w = wb[4 + m % 2]
                    stream_wbf("sp", w, W1b[il], m, 8, tag=f"w1{s_i}.{il}.{hf}.{m}",
                               rkeys=wdk("w1", il, 8))
                    for t2 in range(HT // 512):
                        t0 = t2 * 512
                        mms = []
                        for k in range(8):
                            def mk(k=k, w=w, t0=t0, t2=t2):
                                return lambda e: e.matmul(pbk[t2 % 2][:], w[:, k * P:(k + 1) * P],
                                                          arena[:, XW + k * 1024 + t0:XW + k * 1024 + t0 + 512],
                                                          start=(k == 0), stop=(k == 7))
                            mms.append(mk())
                        pr.op("ten", mms, reads=[(w.name, ALL)] + sum([ab(XW + c * 1024 + t0, 512) for c in range(8)], []),
                              writes=[("psum", t2 % 2)], tag=f"w1m{s_i}.{il}.{hf}.{m}.{t2}")
                        bcol = lnw[:, m:m + 1]
                        if m < KI:
                            pr.op("dve", (lambda m=m, t0=t0, t2=t2, bcol=bcol: lambda e: e.tensor_scalar(
                                arena[:, AB + m * 1024 + t0:AB + m * 1024 + t0 + 512],
                                pbk[t2 % 2][:], bcol, None, ALU.add))(),
                                reads=[("psum", t2 % 2), ("lnw", "b1")], writes=ab(AB + m * 1024 + t0, 512),
                                tag=f"w1a{s_i}.{il}.{hf}.{m}.{t2}")
                        else:
                            mg = m - KI
                            pr.op("act", (lambda t2=t2, bcol=bcol: lambda e: e.activation(
                                qbb[t2][:], pbk[t2 % 2][:], AF.Gelu, bias=bcol))(),
                                reads=[("psum", t2 % 2), ("lnw", "b1")], writes=[(qbb[t2].name, ALL)],
                                tag=f"w1g{s_i}.{il}.{hf}.{m}.{t2}")
                            pr.op("pool", (lambda mg=mg, t0=t0, t2=t2: lambda e: e.tensor_mul(
                                arena[:, AB + mg * 1024 + t0:AB + mg * 1024 + t0 + 512],
                                arena[:, AB + mg * 1024 + t0:AB + mg * 1024 + t0 + 512], qbb[t2][:]))(),
                                reads=[(qbb[t2].name, ALL)] + ab(AB + mg * 1024 + t0, 512),
                                writes=ab(AB + mg * 1024 + t0, 512),
                                tag=f"w1p{s_i}.{il}.{hf}.{m}.{t2}")
                for m in range(8):
                    src = W2b[il].rearrange("(k p) (m q) -> p m k q", p=P, q=P)[:, m, :, :]
                    pr.dma("sp", wbig[:].rearrange("p (k q) -> p k q", q=P), src,
                           reads=wdk("w2", il, KI), writes=[("wbig", ALL)],
                           tag=f"w2{s_i}.{il}.{hf}.{m}")
                    for t2 in range(HT // 512):
                        t0 = t2 * 512
                        mms = []
                        for k in range(KI):
                            def mk(k=k, t0=t0, t2=t2):
                                return lambda e: e.matmul(pbk[2 + t2 % 2][:], wbig[:, k * P:(k + 1) * P],
                                                          arena[:, AB + k * 1024 + t0:AB + k * 1024 + t0 + 512],
                                                          start=(k == 0), stop=(k == KI - 1))
                            mms.append(mk())
                        pr.op("ten", mms,
                              reads=[("wbig", ALL)] + sum([ab(AB + k * 1024 + t0, 512) for k in range(KI)], []),
                              writes=[("psum", 2 + t2 % 2)], tag=f"w2m{s_i}.{il}.{hf}.{m}.{t2}")
                        b2col = lnw[:, 2 * KI + m:2 * KI + m + 1]
                        pr.op("dve", (lambda m=m, t0=t0, t2=t2, b2col=b2col, h0=h0: lambda e: e.scalar_tensor_tensor(
                            xy[:, XC(m, h0 + t0):XC(m, h0 + t0) + 512], pbk[2 + t2 % 2][:], b2col,
                            xy[:, XC(m, h0 + t0):XC(m, h0 + t0) + 512], ALU.add, ALU.add))(),
                            reads=[("psum", 2 + t2 % 2), ("xy", m), ("lnw", "b2")], writes=[("xy", m)],
                            tag=f"w2e{s_i}.{il}.{hf}.{m}.{t2}")
        for c in range(8):
            if STAGE < 7:
                break
            for tt in range(NT):
                t0 = tt * 512
                fb = sq[tt % 2]
                pr.dma("act", fb[:], x1d[c, :, t0:t0 + 512], reads=[("x1d", c)],
                       writes=[("sq", tt % 2)], tag=f"rx{s_i}.{c}.{tt}")
                pr.op("dve", (lambda c=c, t0=t0, fb=fb: lambda e: e.tensor_add(
                    xy[:, XC(c, t0):XC(c, t0) + 512], xy[:, XC(c, t0):XC(c, t0) + 512], fb[:]))(),
                    reads=[("sq", tt % 2), ("xy", c)], writes=[("xy", c)], tag=f"ra{s_i}.{c}.{tt}")

    for s_i in range(NSEQ):
        pr.clear_region("xy")
        for c in range(8):
            pr.dma("sp", xy[:, XC(c, 0):XC(c, 0) + T], xin[s_i, c], writes=[("xy", c)], tag=f"lx{s_i}.{c}")
        if do_l0attn:
            l0attn(s_i)
        if do_l0ffn:
            l0ffn(s_i)
        for c in range(8):
            pr.dma("sp", xout[s_i, c], xy[:, XC(c, 0):XC(c, 0) + T], reads=[("xy", c)], tag=f"sx{s_i}.{c}")

    pr.emit()
    return nc, pr


def host_prep_a(inputs):
    import numpy as np
    NILh = NIL
    i = inputs
    W1 = np.asarray(i["L0_ffn_W1"], np.float32)[:NILh]
    W2 = np.asarray(i["L0_ffn_W2"], np.float32)[:NILh]
    b1 = np.asarray(i["L0_ffn_b1"], np.float32)[:NILh]
    HDI = 2730
    W1p = np.zeros((NILh, 1024, 2 * DI2), np.float32)
    W1p[:, :, 0:HDI] = W1[:, :, 0:HDI]
    W1p[:, :, DI2:DI2 + HDI] = W1[:, :, HDI:2 * HDI]
    W2p = np.zeros((NILh, DI2, 1024), np.float32)
    W2p[:, 0:HDI, :] = W2
    b1pad = np.zeros((NILh, 2 * DI2), np.float32)
    b1pad[:, 0:HDI] = b1[:, 0:HDI]
    b1pad[:, DI2:DI2 + HDI] = b1[:, HDI:]
    b1c = np.ascontiguousarray(b1pad.reshape(NILh, 2 * KI, P).transpose(0, 2, 1))

    def chunkcol(v):
        return np.ascontiguousarray(np.asarray(v, np.float32).reshape(NILh, 8, P).transpose(0, 2, 1))

    qs = np.asarray(i["L0_ffn_q_scale"], np.float32)[:NILh]
    ks = np.asarray(i["L0_ffn_k_scale"], np.float32)[:NILh]
    d64 = np.arange(128) % 64
    perm = (d64 + 32) % 64
    qsd = np.concatenate([qs[:, d64].T, qs[:, perm].T, ks[:, d64].T, ks[:, perm].T], axis=1)

    half = np.arange(0, 64, 2, dtype=np.float64)
    inv_freq = 1.0 / 10000.0 ** (half / 64.0)
    t = np.arange(T, dtype=np.float64)
    fr32 = t[:, None] * inv_freq[None, :]
    base = (half + 0.4 * 64) / (1.4 * 64)
    power = (t - T // 2) / (W // 2)
    sc32 = base[None, :] ** power[:, None]
    fr = fr32[:, d64 % 32].T
    sc = sc32[:, d64 % 32].T
    TABS = np.stack([np.cos(fr) * sc, np.sin(fr) * sc,
                     np.cos(fr) / sc, np.sin(fr) / sc]).astype(np.float32)

    RT = np.zeros((P, P), np.float32)
    for o in (0, 64):
        for d in range(64):
            pp = o + (d + 32) % 64
            RT[pp, o + d] = -1.0 if d < 32 else 1.0

    r = np.arange(P)[:, None]
    cc = np.arange(P)[None, :]
    cmask = np.where(cc <= r, 0.0, -3.0e38).astype(np.float32)
    c512 = np.arange(512)[None, :]
    wm0 = np.where((c512 >= r) & (c512 <= 256 + r), 0.0, -3.0e38).astype(np.float32)
    wm1 = np.where((c512 >= 128 + r) & (c512 <= 384 + r), 0.0, -3.0e38).astype(np.float32)
    c256 = np.arange(256)[None, :]
    wz0 = np.where(c256 <= r, 0.0, -3.0e38).astype(np.float32)
    wz1 = np.where(c256 <= 128 + r, 0.0, -3.0e38).astype(np.float32)
    wmaskd = np.concatenate([wm0, wm1, wz0, wz1], axis=1)

    sel2 = np.zeros((2, 3 * P), np.float32)
    sel2[0, 0:P] = 1.0
    sel2[1, P:2 * P] = 1.0
    for pcol in range(P):
        sel2[pcol // 64, 2 * P + pcol] = 1.0

    im = {
        "wkvd": np.asarray(i["L0_attn_w_kv_down"], np.float32),
        "wkvu": np.asarray(i["L0_attn_w_kv_up"], np.float32),
        "wq": np.asarray(i["L0_attn_w_q"], np.float32),
        "wout": np.asarray(i["L0_attn_w_out"], np.float32),
        "Wqkv6": np.asarray(i["L0_ffn_Wqkv"], np.float32)[:NILh],
        "Wo6": np.asarray(i["L0_ffn_Wo"], np.float32)[:NILh],
        "W1p": W1p, "W2p": W2p, "b1cd": b1c,
        "b2cd": chunkcol(np.asarray(i["L0_ffn_b2"])[:NILh]),
        "lngd": chunkcol(np.asarray(i["L0_ffn_ln_g"])[:NILh]),
        "lnbd": chunkcol(np.asarray(i["L0_ffn_ln_b"])[:NILh]),
        "qsd": qsd,
        "TABS": TABS, "RT2d": RT, "identfd": np.eye(P, dtype=np.float32),
        "cmaskd": cmask, "wmaskd": wmaskd, "sel2d": sel2,
    }
    return im


def pack_xin_a(xseqs):
    import numpy as np
    n = xseqs.shape[0]
    return np.ascontiguousarray(np.asarray(xseqs, np.float32).transpose(0, 2, 1).reshape(n, 8, P, T))


def unpack_xout_a(xo):
    import numpy as np
    n = xo.shape[0]
    return np.ascontiguousarray(xo.reshape(n, 1024, T).transpose(0, 2, 1))


"""Phase B program: L1_attn (rel-pos attention over t=32) + L1_ffn (8-expert dense MoE).

Per-core tokens: NB b-positions x 32 t; feature-major, col = b_local*32 + t.
x3 = x2 + attn(LN(x2)) @ WoT + bo ;  x4 = x3 + LN(x3) + LN_na(moe(LN(x3))).

Segments of SEGB=16 b's (SEGT=512 tokens); 4 head-groups (bg of 4 b's) per segment.
"""
LITE = False
BSTAGE = 9
P = 128
SEGB = 16
SEGT = SEGB * 32
MCH = 32
LN_EPS = 1e-5
GATE_EPS = 1e-6


def build_phase_b(NB=256, NE=8, do_attn=True, do_ffn=True):
    NSEG = NB // SEGB
    NTOK = NB * 32

    nc = bass.Bass("TRN2", target_bir_lowering=False)

    xin = nc.dram_tensor("xin", [8, P, NTOK], F32, kind="ExternalInput").ap()
    wqT = nc.dram_tensor("wqT", [1024, 1024], F32, kind="ExternalInput").ap()
    wkT = nc.dram_tensor("wkT", [1024, 1024], F32, kind="ExternalInput").ap()
    wvT = nc.dram_tensor("wvT", [1024, 1024], F32, kind="ExternalInput").ap()
    woT = nc.dram_tensor("woT", [1024, 1024], F32, kind="ExternalInput").ap()
    bqkv = nc.dram_tensor("bqkv", [P, 32], F32, kind="ExternalInput").ap()
    rkTp = nc.dram_tensor("rkTp", [P, 66], F32, kind="ExternalInput").ap()
    rvt = nc.dram_tensor("rvt", [P, 64], F32, kind="ExternalInput").ap()
    gwT = nc.dram_tensor("gwT", [1024, 8], F32, kind="ExternalInput").ap()
    ebias = nc.dram_tensor("ebias", [P, 8], F32, kind="ExternalInput").ap()
    wupT = nc.dram_tensor("wupT", [NE, 1024, 4096], F32, kind="ExternalInput").ap()
    wdnT = nc.dram_tensor("wdnT", [NE, 4096, 1024], F32, kind="ExternalInput").ap()
    identf = nc.dram_tensor("identf", [P, P], F32, kind="ExternalInput").ap()
    bmaskd = nc.dram_tensor("bmaskd", [P, 32], F32, kind="ExternalInput").ap()
    lmaskd = nc.dram_tensor("lmaskd", [P, 32], F32, kind="ExternalInput").ap()
    hmaskd = nc.dram_tensor("hmaskd", [P, 32], F32, kind="ExternalInput").ap()
    pbmaskd = nc.dram_tensor("pbmaskd", [P, 33], F32, kind="ExternalInput").ap()
    seld = nc.dram_tensor("seld", [8, 8 * P], F32, kind="ExternalInput").ap()
    xout = nc.dram_tensor("xout", [8, P, NTOK], F32, kind="ExternalOutput").ap()

    Gd = nc.dram_tensor("Gd", [2, P, 16 * 33], F32, kind="Internal").ap()
    Pd = nc.dram_tensor("Pd", [2, P, 544], F32, kind="Internal").ap()
    wupTb = nc.dram_tensor("wupTb", [NE, 1024, 4096], BF16, kind="Internal").ap()
    wdnTb = nc.dram_tensor("wdnTb", [NE, 4096, 1024], BF16, kind="Internal").ap()

    S8 = SEGT
    xr = nc.alloc_sbuf_tensor("xr", [P, 8 * S8], F32)
    xln = nc.alloc_sbuf_tensor("xln", [P, 8 * S8], BF16)
    arena = nc.alloc_sbuf_tensor("arena", [P, 32 * 512], BF16)
    attn_fm = nc.alloc_sbuf_tensor("attn_fm", [P, 8 * S8], BF16)
    acc = nc.alloc_sbuf_tensor("acc", [P, 8 * S8], F32)
    NWB = 6
    wb = [nc.alloc_sbuf_tensor(f"wb{i}", [P, 1024], BF16) for i in range(NWB)]
    swb = nc.alloc_sbuf_tensor("swb", [P, S8], F32)
    swf = nc.alloc_sbuf_tensor("swf", [8, S8], F32)
    onescol = nc.alloc_sbuf_tensor("onescol", [1, P], F32)
    onesk = nc.alloc_sbuf_tensor("onesk", [P, 1], F32)
    identb = nc.alloc_sbuf_tensor("identb", [P, P], BF16)
    identf_sb = nc.alloc_sbuf_tensor("identf_sb", [P, P], F32)
    biases = nc.alloc_sbuf_tensor("biases", [P, 32], F32)
    rkT_sb = nc.alloc_sbuf_tensor("rkT_sb", [P, 66], BF16)
    rvt_sb = nc.alloc_sbuf_tensor("rvt_sb", [P, 64], BF16)
    gw_sb = nc.alloc_sbuf_tensor("gw_sb", [P, 8 * 8], F32)
    eb_sb = nc.alloc_sbuf_tensor("eb_sb", [P, 8], F32)
    bm_sb = nc.alloc_sbuf_tensor("bm_sb", [P, 32], F32)
    lm_sb = nc.alloc_sbuf_tensor("lm_sb", [P, 32], F32)
    hm_sb = nc.alloc_sbuf_tensor("hm_sb", [P, 32], F32)
    pbm_sb = nc.alloc_sbuf_tensor("pbm_sb", [P, 33], F32)
    Pf = [nc.alloc_sbuf_tensor(f"Pf{i}", [P, 512], F32) for i in range(2)]
    PfT = [nc.alloc_sbuf_tensor(f"PfT{i}", [P, 512], F32) for i in range(2)]
    PTbf = [nc.alloc_sbuf_tensor(f"PTbf{i}", [P, 512], BF16) for i in range(2)]
    Gsb = [nc.alloc_sbuf_tensor(f"Gsb{i}", [P, 16 * 33], F32) for i in range(2)]
    Bfix = [nc.alloc_sbuf_tensor(f"Bfix{i}", [P, 512], F32) for i in range(2)]
    PB = [nc.alloc_sbuf_tensor(f"PB{i}", [P, 16 * 33], F32) for i in range(2)]
    PBbf = [nc.alloc_sbuf_tensor(f"PBbf{i}", [P, 16 * 33], BF16) for i in range(2)]
    PBT = nc.alloc_sbuf_tensor("PBT", [P, 16 * P], BF16)
    vT = [nc.alloc_sbuf_tensor(f"vT{i}", [P, 8 * P], BF16) for i in range(2)]
    Ysb = [nc.alloc_sbuf_tensor(f"Ysb{i}", [P, 1024], BF16) for i in range(2)]
    nmax = [nc.alloc_sbuf_tensor(f"nmax{i}", [P, 16], F32) for i in range(2)]
    rsum = [nc.alloc_sbuf_tensor(f"rsum{i}", [P, 16], F32) for i in range(2)]
    rrec = [nc.alloc_sbuf_tensor(f"rrec{i}", [P, 16], F32) for i in range(2)]
    tmpA = [nc.alloc_sbuf_tensor(f"tmpA{i}", [P, 512], F32) for i in range(2)]
    tmpB = [nc.alloc_sbuf_tensor(f"tmpB{i}", [P, 512], F32) for i in range(2)]
    ftmp = [nc.alloc_sbuf_tensor(f"ftmp{i}", [P, 512], F32) for i in range(2)]
    sq = [nc.alloc_sbuf_tensor(f"sq{i}", [P, 512], F32) for i in range(2)]
    strow = nc.alloc_sbuf_tensor("strow", [1, 6 * S8], F32)
    gl = nc.alloc_sbuf_tensor("gl", [P, 96], F32)
    spw = nc.alloc_sbuf_tensor("spw", [P, 8 * (S8 // P)], F32)
    zer16 = nc.alloc_sbuf_tensor("zer16", [P, 16], F32)
    epsc = nc.alloc_sbuf_tensor("epsc", [P, 1], F32)
    sel_sb = nc.alloc_sbuf_tensor("sel_sb", [8, 8 * P], F32)
    xlnf = nc.alloc_sbuf_tensor("xlnf", [P, 8 * SEGT], F32)

    pbk = [nc.alloc_psum_tensor(f"pbk{i}", [P, 512], F32) for i in range(8)]
    pT = [pbk[6][:].bitcast(BF16), pbk[7][:].bitcast(BF16)]

    print("sbuf bytes remaining:", nc.sbuf_bytes_remaining)

    pr = Prog(nc)

    pr.op("dve", lambda e: e.memset(onescol[:], 1.0), writes=["onescol"])
    pr.op("dve", lambda e: e.memset(onesk[:], 1.0), writes=["onesk"])
    pr.op("dve", lambda e: e.memset(zer16[:], 0.0), writes=["zer16"])
    pr.op("dve", lambda e: e.memset(epsc[:], LN_EPS), writes=["epsc"])
    pr.dma("pool", identb[:], identf, writes=["identb"])
    pr.dma("sp", identf_sb[:], identf, writes=["identf_sb"])
    pr.dma("sp", biases[:], bqkv, writes=["biases"])
    pr.dma("pool", rkT_sb[:], rkTp, writes=["rkT_sb"])
    pr.dma("pool", rvt_sb[:], rvt, writes=["rvt_sb"])
    pr.dma("pool", gw_sb[:].rearrange("p (c q) -> p c q", q=8),
           gwT.rearrange("(c p) q -> p c q", p=P), writes=["gw_sb"])
    pr.dma("sp", eb_sb[:], ebias, writes=["eb_sb"])
    pr.dma("sp", bm_sb[:], bmaskd, writes=["bm_sb"])
    pr.dma("sp", lm_sb[:], lmaskd, writes=["lm_sb"])
    pr.dma("sp", hm_sb[:], hmaskd, writes=["hm_sb"])
    pr.dma("sp", pbm_sb[:], pbmaskd, writes=["pbm_sb"])
    pr.dma("sp", sel_sb[:], seld, writes=["sel_sb"])
    pr.op("pool", lambda e: e.memset(PBT[:], 0.0), writes=[("PBT", ALL)], tag="pbtz")
    if LITE:
        for hh in range(16):
            pr.op("dve", (lambda hh=hh: lambda e: e.memset(PBT[0:33, hh * 128:(hh + 1) * 128], 0.0))(),
                  writes=[("PBT", hh)], tag=f"pbtz{hh}")
    for par in range(2):
        pr.dma("sp", Pd[par, :, 0:16], zer16[:], reads=["zer16"], writes=[("Pdr", par)], tag=f"pg{par}")
        pr.dma("sp", Pd[par, :, 528:544], zer16[:], reads=["zer16"], writes=[("Pdr", par)], tag=f"pg2{par}")

    if do_ffn:
        pr.clear_region("arena")
        cbv = [arena[:, 0:8192], arena[:, 8192:16384]]
        ci = 0
        for e_i in range(NE):
            for r in range(4):
                b = cbv[ci % 2]
                src = wupT[e_i, r * 2 * P:(r + 1) * 2 * P, :].rearrange("(q p) c -> p q c", p=P)
                pr.dma("pool", b.rearrange("p (q c) -> p q c", c=4096), src,
                       writes=[("arena", ("cb", ci % 2))], tag=f"pcu{e_i}.{r}")
                pr.dma("sp", wupTb[e_i, r * 2 * P:(r + 1) * 2 * P, :].rearrange("(q p) c -> p q c", p=P),
                       b.rearrange("p (q c) -> p q c", c=4096),
                       reads=[("arena", ("cb", ci % 2))], tag=f"pcu2{e_i}.{r}")
                ci += 1
            for r in range(4):
                b = cbv[ci % 2]
                src = wdnT[e_i, r * 8 * P:(r + 1) * 8 * P, :].rearrange("(q p) c -> p q c", p=P)
                pr.dma("pool", b.rearrange("p (q c) -> p q c", c=1024), src,
                       writes=[("arena", ("cb", ci % 2))], tag=f"pcd{e_i}.{r}")
                pr.dma("sp", wdnTb[e_i, r * 8 * P:(r + 1) * 8 * P, :].rearrange("(q p) c -> p q c", p=P),
                       b.rearrange("p (q c) -> p q c", c=1024),
                       reads=[("arena", ("cb", ci % 2))], tag=f"pcd2{e_i}.{r}")
                ci += 1

    def stream_w(q, dst, dsrc, m, tag=""):
        src = dsrc.rearrange("(k p) (m q) -> p m k q", p=P, q=P)[:, m, :, :]
        pr.dma(q, dst[:].rearrange("p (k q) -> p k q", q=P), src,
               writes=[(dst.name, ALL)], tag=tag)

    def ln_fm(src_reads, src_fn, dst, dst_key, stats_off, tag="ln", dstf=None):
        mms = []
        for c in range(8):
            def mk(c=c):
                return lambda e: e.matmul(pbk[4][0:1, 0:512], onesk[:, 0:1],
                                          src_fn(c, 0, 512), start=(c == 0), stop=(c == 7))
            mms.append(mk())
        pr.op("ten", mms, reads=list(src_reads) + ["onesk"], writes=[("psum", 4)], tag=f"{tag}.sm")
        for c in range(8):
            sqb = sq[c % 2]
            pr.op("act", (lambda c=c, sqb=sqb: lambda e: e.square(sqb[:], src_fn(c, 0, 512)))(),
                  reads=list(src_reads), writes=[("sq", c % 2)], tag=f"{tag}.sq{c}")
            pr.op("ten", (lambda c=c, sqb=sqb: lambda e: e.matmul(
                pbk[5][0:1, 0:512], onesk[:, 0:1], sqb[:], start=(c == 0), stop=(c == 7)))(),
                reads=[("sq", c % 2), "onesk"], writes=[("psum", 5)], tag=f"{tag}.s2{c}")
        so, ro, to, t2o = stats_off * S8, (stats_off + 1) * S8, 4 * S8, 5 * S8
        pr.op("dve", (lambda: lambda e: e.tensor_scalar_mul(
            strow[0:1, so:so + 512], pbk[4][0:1, 0:512], 1.0 / 1024.0))(),
            reads=[("psum", 4)], writes=[("strow", stats_off)], tag=f"{tag}.mu")
        pr.op("dve", (lambda: lambda e: e.tensor_scalar_mul(
            strow[0:1, to:to + 512], pbk[5][0:1, 0:512], 1.0 / 1024.0))(),
            reads=[("psum", 5)], writes=[("strow", 4)], tag=f"{tag}.ms")
        pr.op("act", (lambda: lambda e: e.square(strow[0:1, t2o:t2o + 512], strow[0:1, so:so + 512]))(),
              reads=[("strow", stats_off)], writes=[("strow", 5)], tag=f"{tag}.m2")
        pr.op("dve", (lambda: lambda e: e.tensor_sub(
            strow[0:1, to:to + 512], strow[0:1, to:to + 512], strow[0:1, t2o:t2o + 512]))(),
            reads=[("strow", 5), ("strow", 4)], writes=[("strow", 4)], tag=f"{tag}.v")
        pr.op("act", (lambda: lambda e: e.activation(
            strow[0:1, ro:ro + 512], strow[0:1, to:to + 512], AF.Sqrt, bias=epsc[0:1, 0:1]))(),
            reads=[("strow", 4), "epsc"], writes=[("strow", stats_off + 1)], tag=f"{tag}.sd")
        pr.op("dve", (lambda: lambda e: e.reciprocal(
            strow[0:1, ro:ro + 512], strow[0:1, ro:ro + 512]))(),
            reads=[("strow", stats_off + 1)], writes=[("strow", stats_off + 1)], tag=f"{tag}.rc")
        if dst is None:
            return
        pr.op("ten", (lambda: lambda e: e.matmul(pbk[4][:, 0:512], onescol[0:1, :],
            strow[0:1, so:so + 512], start=True, stop=True))(),
            reads=[("strow", stats_off), "onescol"], writes=[("psum", 4)], tag=f"{tag}.b1")
        pr.op("ten", (lambda: lambda e: e.matmul(pbk[5][:, 0:512], onescol[0:1, :],
            strow[0:1, ro:ro + 512], start=True, stop=True))(),
            reads=[("strow", stats_off + 1), "onescol"], writes=[("psum", 5)], tag=f"{tag}.b2")
        for c in range(8):
            sqb = sq[c % 2]
            pr.op("dve", (lambda c=c, sqb=sqb: lambda e: e.tensor_sub(
                sqb[:], src_fn(c, 0, 512), pbk[4][:, 0:512]))(),
                reads=list(src_reads) + [("psum", 4)], writes=[("sq", c % 2)], tag=f"{tag}.c{c}")
            if dstf is None:
                pr.op("dve", (lambda c=c, sqb=sqb: lambda e: e.tensor_mul(
                    dst[:, c * S8:c * S8 + 512], sqb[:], pbk[5][:, 0:512]))(),
                    reads=[("sq", c % 2), ("psum", 5)], writes=[(dst_key, c)], tag=f"{tag}.n{c}")
            else:
                pr.op("dve", (lambda c=c, sqb=sqb: lambda e: e.tensor_mul(
                    dstf[:, c * S8:c * S8 + 512], sqb[:], pbk[5][:, 0:512]))(),
                    reads=[("sq", c % 2), ("psum", 5)], writes=[("xlnf", c)], tag=f"{tag}.nf{c}")
                pr.op("pool", (lambda c=c: lambda e: e.tensor_copy(
                    dst[:, c * S8:c * S8 + 512], dstf[:, c * S8:c * S8 + 512]))(),
                    reads=[("xlnf", c)], writes=[(dst_key, c)], tag=f"{tag}.n{c}")

    QOFF, KOFF, VOFF = 0, 8 * S8, 16 * S8

    def proj_seg(widt, bias_off, dst_off, pkey, scale=None, seg=0, nm=""):
        for m in range(8):
            w = wb[m % 3] if nm != "o" else wb[3 + m % 3]
            stream_w("pool", w, widt, m, tag=f"{nm}w{seg}.{m}")
            pbx = pbk[m % 2]
            src, srckey = (xln, "xln") if nm != "o" else (attn_fm, "attn_fm")
            mms = []
            for k in range(8):
                def mk(k=k, w=w, pbx=pbx, src=src):
                    return lambda e: e.matmul(pbx[:], w[:, k * P:(k + 1) * P],
                                              src[:, k * S8:k * S8 + 512],
                                              start=(k == 0), stop=(k == 7))
                mms.append(mk())
            pr.op("ten", mms, reads=[(w.name, ALL)] + [(srckey, k) for k in range(8)],
                  writes=[("psum", m % 2)], tag=f"{nm}mm{seg}.{m}")
            bcol = biases[:, bias_off + m:bias_off + m + 1]
            if nm == "o":
                pr.op("dve", (lambda m=m, pbx=pbx, bcol=bcol: lambda e: e.scalar_tensor_tensor(
                    xr[:, m * S8:m * S8 + 512], pbx[:], bcol,
                    xr[:, m * S8:m * S8 + 512], ALU.add, ALU.add))(),
                    reads=[("psum", m % 2), ("xr", m), "biases"], writes=[("xr", m)],
                    tag=f"{nm}ev{seg}.{m}")
            elif scale is None:
                pr.op("dve", (lambda m=m, pbx=pbx, bcol=bcol: lambda e: e.tensor_scalar(
                    arena[:, dst_off + m * S8: dst_off + m * S8 + 512],
                    pbx[:], bcol, None, ALU.add))(),
                    reads=[("psum", m % 2), "biases"], writes=[("arena", (pkey, m))],
                    tag=f"{nm}ev{seg}.{m}")
            else:
                pr.op("dve", (lambda m=m, pbx=pbx, bcol=bcol, scale=scale: lambda e: e.tensor_scalar(
                    arena[:, dst_off + m * S8: dst_off + m * S8 + 512],
                    pbx[:], bcol, scale, ALU.add, ALU.mult))(),
                    reads=[("psum", m % 2), "biases"], writes=[("arena", (pkey, m))],
                    tag=f"{nm}ev{seg}.{m}")

    for seg in range(NSEG):
        segc0 = seg * SEGT
        for c in range(8):
            pr.dma("sp", xr[:, c * S8:(c + 1) * S8], xin[c, :, segc0:segc0 + SEGT],
                   writes=[("xr", c)], tag=f"ldx{seg}.{c}")

        if do_attn:
            pr.clear_region("xln")
            ln_fm([("xr", c) for c in range(8)],
                  lambda c, t0, n: xr[:, c * S8 + t0:c * S8 + t0 + n],
                  xln, "xln", 0, tag=f"lnA{seg}")
            pr.clear_region("arena")
            proj_seg(wqT, 0, QOFF, "q", scale=0.125, seg=seg, nm="q")
            proj_seg(wkT, 8, KOFF, "k", seg=seg, nm="k")
            proj_seg(wvT, 16, VOFF, "v", seg=seg, nm="v")

            if 0 <= BSTAGE <= 4:
                for ch in range(8):
                    pr.op("dve", (lambda ch=ch: lambda e: e.memset(attn_fm[:, ch * S8:(ch + 1) * S8], 0.0))(),
                          writes=[("attn_fm", ch)], tag=f"afz{seg}.{ch}")
            for bg in range(4 if BSTAGE >= 1 else 0):
                par = bg % 2
                C0 = bg * 128
                smms = []
                for hb in (0, 64):
                    for j in range(4):
                        for h in range(hb // 64, 16, 2):
                            ch = h // 2
                            def mk(h=h, j=j, ch=ch, hb=hb, par=par, C0=C0):
                                qo = QOFF + ch * S8 + C0 + 32 * j
                                ko = KOFF + ch * S8 + C0 + 32 * j
                                return lambda e: e.matmul(
                                    pbk[2 + par][32 * j:32 * j + 32, h * 32:h * 32 + 32],
                                    arena[hb:hb + 64, qo:qo + 32],
                                    arena[hb:hb + 64, ko:ko + 32],
                                    start=True, stop=True, tile_position=(hb, 32 * j))
                            smms.append(mk())
                pr.op("ten", smms,
                      reads=[("arena", ("q", c)) for c in range(8)] + [("arena", ("k", c)) for c in range(8)],
                      writes=[("psum", 2 + par)], tag=f"S{seg}.{bg}")
                for rnd in range(2 if not LITE else 0):
                    gmms = []
                    for hh in range(8):
                        h = rnd * 8 + hh
                        ch, hb = h // 2, 64 * (h % 2)
                        def mkg(h=h, hh=hh, ch=ch, hb=hb, C0=C0):
                            qo = QOFF + ch * S8 + C0
                            ro = 33 * (h % 2)
                            return lambda e: e.matmul(
                                pbk[4][0:P, hh * 64:hh * 64 + 33],
                                arena[0:P, qo:qo + 128],
                                rkT_sb[0:P, ro:ro + 33],
                                start=True, stop=True)
                        gmms.append(mkg())
                    pr.op("ten", gmms,
                          reads=[("arena", ("q", c)) for c in range(8)] + ["rkT_sb"],
                          writes=[("psum", 4)], tag=f"G{seg}.{bg}.{rnd}")
                    gsrc = bass.AP(pbk[4][:].tensor, 0, [[512, P], [64, 8], [1, 33]])
                    gdst = bass.AP(Gsb[par][:].tensor, rnd * 8 * 33, [[528, P], [33, 8], [1, 33]])
                    pr.op("dve", (lambda gsrc=gsrc, gdst=gdst: lambda e: e.tensor_copy(gdst, gsrc))(),
                          reads=[("psum", 4)], writes=[("Gsb", par)], tag=f"Gev{seg}.{bg}.{rnd}")
                if not LITE:
                    pr.dma("act", Gd[par], Gsb[par][:], reads=[("Gsb", par)], writes=[("Gd", par)], tag=f"Gout{seg}.{bg}")
                for j in range(4 if not LITE else 0):
                    bnd = bass.AP(Gd.tensor, par * P * 528 + j * 16896 + 16,
                                  [[527, 32], [33, 16], [1, 32]])
                    pr.dma("act", tmpA[par][32 * j:32 * j + 32, :], bnd,
                           reads=[("Gd", par)], writes=[("tmpA", par)], tag=f"Gin{seg}.{bg}.{j}")
                bmv = bass.AP(bm_sb[:].tensor, 0, [[32, P], [0, 16], [1, 32]])
                lmv = bass.AP(lm_sb[:].tensor, 0, [[32, P], [0, 16], [1, 32]])
                hmv = bass.AP(hm_sb[:].tensor, 0, [[32, P], [0, 16], [1, 32]])
                g0v = bass.AP(Gsb[par][:].tensor, 0, [[528, P], [33, 16], [0, 32]])
                g32v = bass.AP(Gsb[par][:].tensor, 32, [[528, P], [33, 16], [0, 32]])
                tav = bass.AP(tmpA[par][:].tensor, 0, [[512, P], [32, 16], [1, 32]])
                bfv = bass.AP(Bfix[par][:].tensor, 0, [[512, P], [32, 16], [1, 32]])
                tbv = bass.AP(tmpB[par][:].tensor, 0, [[512, P], [32, 16], [1, 32]])
                if LITE:
                    pr.op("pool", (lambda par=par: lambda e: e.memset(Bfix[par][:], 0.0))(),
                          writes=[("Bfix", par)], tag=f"bfz.{seg}.{bg}")
                else:
                    pr.op("pool", (lambda a=tav, b=bmv, o=bfv: lambda e: e.tensor_mul(o, a, b))(),
                      reads=[("tmpA", par), "bm_sb"], writes=[("Bfix", par)], tag=f"bf1.{seg}.{bg}")
                if not LITE:
                    pr.op("dve", (lambda a=g0v, b=lmv, o=tbv: lambda e: e.tensor_mul(o, a, b))(),
                          reads=[("Gsb", par), "lm_sb"], writes=[("tmpB", par)], tag=f"bf2.{seg}.{bg}")
                if not LITE:
                    pr.op("pool", (lambda o=bfv, b=tbv: lambda e: e.tensor_add(o, o, b))(),
                          reads=[("tmpB", par), ("Bfix", par)], writes=[("Bfix", par)], tag=f"bf3.{seg}.{bg}")
                if not LITE:
                    pr.op("dve", (lambda a=g32v, b=hmv, o=tbv: lambda e: e.tensor_mul(o, a, b))(),
                          reads=[("Gsb", par), "hm_sb"], writes=[("tmpB", par)], tag=f"bf4.{seg}.{bg}")
                if not LITE:
                    pr.op("pool", (lambda o=bfv, b=tbv: lambda e: e.tensor_add(o, o, b))(),
                          reads=[("tmpB", par), ("Bfix", par)], writes=[("Bfix", par)], tag=f"bf5.{seg}.{bg}")
                pr.op("dve", (lambda par=par: lambda e: e.tensor_add(
                    Bfix[par][:], pbk[2 + par][:], Bfix[par][:]))(),
                    reads=[("psum", 2 + par), ("Bfix", par)], writes=[("Bfix", par)], tag=f"st.{seg}.{bg}")
                pr.op("dve", (lambda par=par: lambda e: e.reduce_max(
                    nmax[par][:], Bfix[par][:].rearrange("p (h s) -> p h s", s=32), axis=AX.X, negate=True))(),
                    reads=[("Bfix", par)], writes=[("nmax", par)], tag=f"mx.{seg}.{bg}")
                nmv = bass.AP(nmax[par][:].tensor, 0, [[16, P], [1, 16], [0, 32]])
                bf3v = bass.AP(Bfix[par][:].tensor, 0, [[512, P], [32, 16], [1, 32]])
                ta3v = bass.AP(tmpA[par][:].tensor, 0, [[512, P], [32, 16], [1, 32]])
                pr.op("pool", (lambda par=par, nmv=nmv, bf3v=bf3v, ta3v=ta3v: lambda e: e.tensor_add(
                    ta3v, bf3v, nmv))(),
                    reads=[("Bfix", par), ("nmax", par)], writes=[("tmpA", par)], tag=f"sh.{seg}.{bg}")
                pr.op("act", (lambda par=par: lambda e: e.activation(Pf[par][:], tmpA[par][:], AF.Exp))(),
                      reads=[("tmpA", par)], writes=[("Pf", par)], tag=f"ex.{seg}.{bg}")
                pr.op("dve", (lambda par=par: lambda e: e.reduce_sum(
                    rsum[par][:], Pf[par][:].rearrange("p (h s) -> p h s", s=32), axis=AX.X))(),
                    reads=[("Pf", par)], writes=[("rsum", par)], tag=f"rs.{seg}.{bg}")
                pr.op("dve", (lambda par=par: lambda e: e.reciprocal(rrec[par][:], rsum[par][:]))(),
                      reads=[("rsum", par)], writes=[("rrec", par)], tag=f"rr.{seg}.{bg}")
                if BSTAGE < 2:
                    continue
                pr.op("dve", (lambda par=par: lambda e: e.transpose(PfT[par][:], Pf[par][:]))(),
                      reads=[("Pf", par)], writes=[("PfT", par)], tag=f"pt.{seg}.{bg}")
                pr.op("pool", (lambda par=par: lambda e: e.tensor_copy(PTbf[par][:], PfT[par][:]))(),
                      reads=[("PfT", par)], writes=[("PTbf", par)], tag=f"ptc.{seg}.{bg}")
                vtm = []
                for c in range(8):
                    def mkv(c=c, C0=C0, par=par):
                        vo = VOFF + c * S8 + C0
                        return lambda e: e.transpose(pT[par][:, c * 128:(c + 1) * 128],
                                                     arena[:, vo:vo + 128], identb[:])
                    vtm.append(mkv())
                pr.op("ten", vtm, reads=[("arena", ("v", c)) for c in range(8)] + ["identb"],
                      writes=[("psum", 6 + par)], tag=f"vt.{seg}.{bg}")
                pr.op("dve", (lambda par=par: lambda e: e.tensor_copy(vT[par][:], pT[par][:]))(),
                      reads=[("psum", 6 + par)], writes=[("vT", par)], tag=f"vte.{seg}.{bg}")
                if not LITE:
                    pr.dma("act", Pd[par, :, 16:528], Pf[par][:], reads=[("Pf", par)], writes=[("Pdr", par)], tag=f"pdo.{seg}.{bg}")
                for j in range(4 if not LITE else 0):
                    pbndj = bass.AP(Pd.tensor, par * P * 544 + j * 17408,
                                    [[545, 32], [32, 16], [1, 33]])
                    pr.dma("act", PB[par][32 * j:32 * j + 32, :], pbndj,
                           reads=[("Pdr", par)], writes=[("PB", par)], tag=f"pbi.{seg}.{bg}.{j}")
                pbmv = bass.AP(pbm_sb[:].tensor, 0, [[33, P], [0, 16], [1, 33]])
                pbv = bass.AP(PB[par][:].tensor, 0, [[528, P], [33, 16], [1, 33]])
                if not LITE:
                    pr.op("pool", (lambda o=pbv, b=pbmv: lambda e: e.tensor_mul(o, o, b))(),
                          reads=[("PB", par), "pbm_sb"], writes=[("PB", par)], tag=f"pbm.{seg}.{bg}")
                pfv = bass.AP(Pf[par][:].tensor, 0, [[512, P], [32, 16], [1, 32]])
                tb3v = bass.AP(tmpB[par][:].tensor, 0, [[512, P], [32, 16], [1, 32]])
                if not LITE:
                    pr.op("dve", (lambda a=pfv, b=lmv, o=tb3v: lambda e: e.tensor_mul(o, a, b))(),
                          reads=[("Pf", par), "lm_sb"], writes=[("tmpB", par)], tag=f"p0a.{seg}.{bg}")
                if not LITE:
                    pr.op("dve", (lambda par=par: lambda e: e.reduce_sum(
                        bass.AP(PB[par][:].tensor, 0, [[528, P], [33, 16]]),
                        tmpB[par][:].rearrange("p (h s) -> p h s", s=32), axis=AX.X))(),
                        reads=[("tmpB", par)], writes=[("PB", par)], tag=f"p0b.{seg}.{bg}")
                if not LITE:
                    pr.op("dve", (lambda a=pfv, b=hmv, o=tb3v: lambda e: e.tensor_mul(o, a, b))(),
                          reads=[("Pf", par), "hm_sb"], writes=[("tmpB", par)], tag=f"p32a.{seg}.{bg}")
                if not LITE:
                    pr.op("dve", (lambda par=par: lambda e: e.reduce_sum(
                        bass.AP(PB[par][:].tensor, 32, [[528, P], [33, 16]]),
                        tmpB[par][:].rearrange("p (h s) -> p h s", s=32), axis=AX.X))(),
                        reads=[("tmpB", par)], writes=[("PB", par)], tag=f"p32b.{seg}.{bg}")
                if not LITE:
                    pr.op("pool", (lambda par=par: lambda e: e.tensor_copy(PBbf[par][:], PB[par][:]))(),
                          reads=[("PB", par)], writes=[("PBbf", par)], tag=f"pbc.{seg}.{bg}")
                for h in range(16 if not LITE else 0):
                    tb = pT[h % 2]
                    slot = (h // 2) % 4
                    pr.op("ten", (lambda h=h, par=par, tb=tb, slot=slot: lambda e: e.transpose(
                        tb[0:33, slot * 128:slot * 128 + 128],
                        PBbf[par][:, h * 33:h * 33 + 33], identb[:]))(),
                        reads=[("PBbf", par), "identb"], writes=[("psum", 6 + h % 2)], tag=f"pbt.{seg}.{bg}.{h}")
                    pr.op("dve", (lambda h=h, tb=tb, slot=slot: lambda e: e.tensor_copy(
                        PBT[0:33, h * 128:h * 128 + 128], tb[0:33, slot * 128:slot * 128 + 128]))(),
                        reads=[("psum", 6 + h % 2)], writes=[("PBT", h)], tag=f"pbte.{seg}.{bg}.{h}")
                if BSTAGE < 3:
                    continue
                for h in range(16):
                    ch = h // 2
                    yb = pbk[4 + h // 8]
                    rb = pbk[2 + (1 - par)]
                    yo = (h % 8) * 64
                    ro = (h % 8) * 64
                    mms = []
                    for j in range(4):
                        def mkp(h=h, j=j, par=par, yb=yb, yo=yo, ch=ch):
                            return lambda e: e.matmul(
                                yb[32 * j:32 * j + 32, yo:yo + 64],
                                PTbf[par][32 * j:32 * j + 32, h * 32:h * 32 + 32],
                                vT[par][32 * j:32 * j + 32, ch * 128 + 64 * (h % 2): ch * 128 + 64 * (h % 2) + 64],
                                start=True, stop=True,
                                tile_position=(32 * j, 32 * j))
                        mms.append(mkp())
                    def mkr(h=h, rb=rb, ro=ro):
                        return lambda e: e.matmul(
                            rb[:, ro:ro + 64], PBT[0:P, h * 128:h * 128 + 128],
                            rvt_sb[0:P, 0:64], start=True, stop=True)
                    mms.append(mkr())
                    pr.op("ten", mms,
                          reads=[("PTbf", par), ("vT", par), ("PBT", h), "rvt_sb"],
                          writes=[("psum", 4 + h // 8), ("psum", 2 + (1 - par))],
                          tag=f"pv.{seg}.{bg}.{h}")
                    pr.op("dve", (lambda h=h, par=par, yb=yb, yo=yo: lambda e: e.tensor_scalar_mul(
                        Ysb[par][:, h * 64:h * 64 + 64], yb[:, yo:yo + 64], rrec[par][:, h:h + 1]))(),
                        reads=[("psum", 4 + h // 8), ("rrec", par)], writes=[("Ysb", (par, h))],
                        tag=f"ye.{seg}.{bg}.{h}")
                    pr.op("dve", (lambda h=h, par=par, rb=rb, ro=ro: lambda e: e.scalar_tensor_tensor(
                        Ysb[par][:, h * 64:h * 64 + 64], rb[:, ro:ro + 64], rrec[par][:, h:h + 1],
                        Ysb[par][:, h * 64:h * 64 + 64], ALU.mult, ALU.add))(),
                        reads=[("psum", 2 + (1 - par)), ("rrec", par), ("Ysb", (par, h))],
                        writes=[("Ysb", (par, h))], tag=f"ye2.{seg}.{bg}.{h}")
                for h in range(16 if BSTAGE >= 4 else 0):
                    ch, hb = h // 2, 64 * (h % 2)
                    tb = pT[h % 2]
                    slot = (h // 2) % 4
                    pr.op("ten", (lambda h=h, par=par, tb=tb, slot=slot: lambda e: e.transpose(
                        tb[0:64, slot * 128:slot * 128 + 128],
                        Ysb[par][:, h * 64:h * 64 + 64], identb[:]))(),
                        reads=[("Ysb", (par, h)), "identb"], writes=[("psum", 6 + h % 2)],
                        tag=f"yt.{seg}.{bg}.{h}")
                    pr.op("act", (lambda h=h, ch=ch, hb=hb, tb=tb, slot=slot, C0=C0: lambda e: e.copy(
                        attn_fm[hb:hb + 64, ch * S8 + C0:ch * S8 + C0 + 128],
                        tb[0:64, slot * 128:slot * 128 + 128]))(),
                        reads=[("psum", 6 + h % 2)], writes=[("attn_fm", ch)], tag=f"yte.{seg}.{bg}.{h}")
            proj_seg(woT, 24, 0, "o", seg=seg, nm="o")

        if do_ffn:
            pr.clear_region("xln")
            pr.clear_region("xlnf")
            ln_fm([("xr", c) for c in range(8)],
                  lambda c, t0, n: xr[:, c * S8 + t0:c * S8 + t0 + n],
                  xln, "xln", 0, tag=f"lnF{seg}", dstf=xlnf)
            for tt in range(SEGT // P):
                t0 = tt * P
                mms = []
                for k in range(8):
                    def mk(k=k, t0=t0):
                        return lambda e: e.matmul(pbk[2][0:P, 0:8], xlnf[:, k * S8 + t0:k * S8 + t0 + P],
                                                  gw_sb[:, k * 8:(k + 1) * 8], start=(k == 0), stop=(k == 7))
                    mms.append(mk())
                pr.op("ten", mms, reads=[("xlnf", c) for c in range(8)] + ["gw_sb"],
                      writes=[("psum", 2)], tag=f"gt{seg}.{tt}")
                L = lambda i: gl[:, i * 8:(i + 1) * 8]
                C = lambda i: gl[:, 80 + i:81 + i]
                pr.op("dve", (lambda: lambda e: e.tensor_add(L(0), pbk[2][0:P, 0:8], eb_sb[:]))(),
                      reads=[("psum", 2), "eb_sb"], writes=[("gl", 0)], tag=f"g0.{seg}.{tt}")
                pr.op("dve", (lambda: lambda e: e.reduce_max(C(0), L(0), axis=AX.X, negate=True))(),
                      reads=[("gl", 0)], writes=[("gl", "m1")], tag=f"g1.{seg}.{tt}")
                pr.op("dve", (lambda: lambda e: e.tensor_scalar(L(1), L(0), C(0), 0.0, ALU.add, ALU.is_ge))(),
                      reads=[("gl", 0), ("gl", "m1")], writes=[("gl", 1)], tag=f"g2.{seg}.{tt}")
                pr.op("dve", (lambda: lambda e: e.scalar_tensor_tensor(L(2), L(1), -1e30, L(0), ALU.mult, ALU.add))(),
                      reads=[("gl", 0), ("gl", 1)], writes=[("gl", 2)], tag=f"g3.{seg}.{tt}")
                pr.op("dve", (lambda: lambda e: e.reduce_max(C(1), L(2), axis=AX.X, negate=True))(),
                      reads=[("gl", 2)], writes=[("gl", "m2")], tag=f"g4.{seg}.{tt}")
                pr.op("dve", (lambda: lambda e: e.tensor_scalar(L(3), L(0), C(1), 0.0, ALU.add, ALU.is_ge))(),
                      reads=[("gl", 0), ("gl", "m2")], writes=[("gl", 3)], tag=f"g5.{seg}.{tt}")
                pr.op("act", (lambda: lambda e: e.activation(L(4), L(0), AF.Exp, bias=C(0), accum_out=C(2)))(),
                      reads=[("gl", 0), ("gl", "m1")], writes=[("gl", 4), ("gl", "z")], tag=f"g6.{seg}.{tt}")
                pr.op("dve", (lambda: lambda e: e.reciprocal(C(3), C(2)))(),
                      reads=[("gl", "z")], writes=[("gl", "rz")], tag=f"g7.{seg}.{tt}")
                pr.op("dve", (lambda: lambda e: e.tensor_scalar_mul(L(4), L(4), C(3)))(),
                      reads=[("gl", 4), ("gl", "rz")], writes=[("gl", 4)], tag=f"g8.{seg}.{tt}")
                pr.op("dve", (lambda: lambda e: e.tensor_mul(L(5), L(4), L(3)))(),
                      reads=[("gl", 4), ("gl", 3)], writes=[("gl", 5)], tag=f"g9.{seg}.{tt}")
                pr.op("dve", (lambda: lambda e: e.reduce_sum(C(4), L(5), axis=AX.X))(),
                      reads=[("gl", 5)], writes=[("gl", "s2")], tag=f"gA.{seg}.{tt}")
                pr.op("dve", (lambda: lambda e: e.tensor_scalar_add(C(4), C(4), GATE_EPS))(),
                      reads=[("gl", "s2")], writes=[("gl", "s2")], tag=f"gB.{seg}.{tt}")
                pr.op("dve", (lambda: lambda e: e.reciprocal(C(5), C(4)))(),
                      reads=[("gl", "s2")], writes=[("gl", "r2")], tag=f"gC.{seg}.{tt}")
                pr.op("dve", (lambda tt=tt: lambda e: e.tensor_scalar_mul(
                    spw[:, tt * 8:(tt + 1) * 8], L(5), C(5)))(),
                    reads=[("gl", 5), ("gl", "r2")], writes=[("spw", tt)], tag=f"gD.{seg}.{tt}")
                pr.op("ten", (lambda tt=tt: lambda e: e.transpose(
                    pbk[3][0:8, 0:P], spw[:, tt * 8:(tt + 1) * 8], identf_sb[:]))(),
                    reads=[("spw", tt), "identf_sb"], writes=[("psum", 3)], tag=f"gT.{seg}.{tt}")
                pr.op("dve", (lambda tt=tt: lambda e: e.tensor_copy(
                    swf[0:8, tt * P:(tt + 1) * P], pbk[3][0:8, 0:P]))(),
                    reads=[("psum", 3)], writes=[("swf", tt)], tag=f"gTe.{seg}.{tt}")
            pr.clear_region("arena")
            for e_i in range(NE):
                pr.op("ten", (lambda e_i=e_i: lambda e: e.matmul(
                    pbk[3][:, 0:512], sel_sb[0:8, e_i * P:(e_i + 1) * P], swf[0:8, 0:512],
                    start=True, stop=True))(),
                    reads=[("swf", ALL), "sel_sb"], writes=[("psum", 3)], tag=f"sb{seg}.{e_i}")
                pr.op("dve", (lambda: lambda e: e.tensor_copy(swb[:, 0:512], pbk[3][:, 0:512]))(),
                      reads=[("psum", 3)], writes=["swb"], tag=f"sbe{seg}.{e_i}")
                for m in range(MCH):
                    w = wb[m % 4]
                    pr.dma("sp", w[:].rearrange("p (k q) -> p k q", q=P),
                           wupTb[e_i].rearrange("(k p) (m q) -> p m k q", p=P, q=P)[:, m, :, :],
                           writes=[(w.name, ALL)], tag=f"uw{seg}.{e_i}.{m}")
                    bi = m % 2
                    mms = []
                    for k in range(8):
                        def mk(k=k, w=w, bi=bi):
                            return lambda e: e.matmul(pbk[bi][:], w[:, k * P:(k + 1) * P],
                                                      xln[:, k * S8:k * S8 + 512],
                                                      start=(k == 0), stop=(k == 7))
                        mms.append(mk())
                    pr.op("ten", mms, reads=[(w.name, ALL)] + [("xln", c) for c in range(8)],
                          writes=[("psum", bi)], tag=f"hm{seg}.{e_i}.{m}")
                    pr.op("act", (lambda m=m, bi=bi: lambda e: e.activation(
                        arena[:, m * 512:(m + 1) * 512], pbk[bi][:], AF.Gelu))(),
                        reads=[("psum", bi)], writes=[("arena", ("h", m))], tag=f"ge{seg}.{e_i}.{m}")
                for m in range(MCH):
                    w = wb[4 + m % 2]
                    pr.dma("sp", w[:].rearrange("p (k q) -> p k q", q=P),
                           wdnTb[e_i].rearrange("(m q) (c p) -> q m c p", q=P, p=P)[:, m, :, :],
                           writes=[(w.name, ALL)], tag=f"dw{seg}.{e_i}.{m}")
                    for c in range(8):
                        def mkd(m=m, c=c, w=w):
                            return lambda e: e.matmul(pbk[c][:], w[:, c * P:(c + 1) * P],
                                                      arena[:, m * 512:(m + 1) * 512],
                                                      start=(m == 0), stop=(m == MCH - 1))
                        pr.op("ten", mkd(), reads=[(w.name, ALL), ("arena", ("h", m))],
                              writes=[("psum", c)], tag=f"dm{seg}.{e_i}.{m}.{c}")
                for c in range(8):
                    fb = ftmp[c % 2]
                    pr.op("dve", (lambda c=c, fb=fb: lambda e: e.tensor_mul(
                        fb[:], pbk[c][:], swb[:, 0:512]))(),
                        reads=[("psum", c), "swb"], writes=[("ftmp", c % 2)], tag=f"dv{seg}.{e_i}.{c}")
                    if e_i == 0:
                        pr.op("pool", (lambda c=c, fb=fb: lambda e: e.tensor_copy(
                            acc[:, c * S8:c * S8 + 512], fb[:]))(),
                            reads=[("ftmp", c % 2)], writes=[("acc", c)], tag=f"da{seg}.{e_i}.{c}")
                    else:
                        pr.op("pool", (lambda c=c, fb=fb: lambda e: e.tensor_add(
                            acc[:, c * S8:c * S8 + 512], acc[:, c * S8:c * S8 + 512], fb[:]))(),
                            reads=[("ftmp", c % 2), ("acc", c)], writes=[("acc", c)], tag=f"da{seg}.{e_i}.{c}")
            ln_fm([("acc", c) for c in range(8)],
                  lambda c, t0, n: acc[:, c * S8 + t0:c * S8 + t0 + n],
                  None, None, 2, tag=f"lnO{seg}")
            for (st_off, bank) in ((0, 2), (1, 3), (2, 4), (3, 5)):
                pr.op("ten", (lambda st_off=st_off, bank=bank: lambda e: e.matmul(
                    pbk[bank][:, 0:512], onescol[0:1, :],
                    strow[0:1, st_off * S8:st_off * S8 + 512], start=True, stop=True))(),
                    reads=[("strow", st_off), "onescol"], writes=[("psum", bank)],
                    tag=f"fb{seg}.{st_off}")
            for c in range(8):
                sqb = sq[c % 2]
                fb = ftmp[c % 2]
                pr.op("dve", (lambda c=c, sqb=sqb: lambda e: e.tensor_sub(
                    sqb[:], xr[:, c * S8:c * S8 + 512], pbk[2][:, 0:512]))(),
                    reads=[("xr", c), ("psum", 2)], writes=[("sq", c % 2)], tag=f"f1{seg}.{c}")
                pr.op("dve", (lambda c=c, sqb=sqb: lambda e: e.tensor_mul(
                    sqb[:], sqb[:], pbk[3][:, 0:512]))(),
                    reads=[("sq", c % 2), ("psum", 3)], writes=[("sq", c % 2)], tag=f"f2{seg}.{c}")
                pr.op("pool", (lambda c=c, sqb=sqb: lambda e: e.tensor_add(
                    xr[:, c * S8:c * S8 + 512], xr[:, c * S8:c * S8 + 512], sqb[:]))(),
                    reads=[("sq", c % 2), ("xr", c)], writes=[("xr", c)], tag=f"f3{seg}.{c}")
                pr.op("dve", (lambda c=c, fb=fb: lambda e: e.tensor_sub(
                    fb[:], acc[:, c * S8:c * S8 + 512], pbk[4][:, 0:512]))(),
                    reads=[("acc", c), ("psum", 4)], writes=[("ftmp", c % 2)], tag=f"f4{seg}.{c}")
                pr.op("dve", (lambda c=c, fb=fb: lambda e: e.tensor_mul(
                    fb[:], fb[:], pbk[5][:, 0:512]))(),
                    reads=[("ftmp", c % 2), ("psum", 5)], writes=[("ftmp", c % 2)], tag=f"f5{seg}.{c}")
                pr.op("pool", (lambda c=c, fb=fb: lambda e: e.tensor_add(
                    xr[:, c * S8:c * S8 + 512], xr[:, c * S8:c * S8 + 512], fb[:]))(),
                    reads=[("ftmp", c % 2), ("xr", c)], writes=[("xr", c)], tag=f"f6{seg}.{c}")
        for c in range(8):
            pr.dma("sp", xout[c, :, segc0:segc0 + SEGT], xr[:, c * S8:(c + 1) * S8],
                   reads=[("xr", c)], tag=f"sx{seg}.{c}")

    pr.emit()
    return nc, pr


def host_prep_b(inputs):
    import numpy as np

    def chunkcol(v):
        return np.ascontiguousarray(v.reshape(8, 128).T).astype(np.float32)

    tp = np.tile(np.arange(32), 4)[:, None]
    s = np.arange(32)[None, :]
    r = np.arange(33)[None, :]
    rkT = np.ascontiguousarray(np.asarray(inputs["L1_attn_rk_table"]).T).astype(np.float32)
    rkTe = np.zeros((128, 33), np.float32); rkTe[0:64] = rkT
    rkTo = np.zeros((128, 33), np.float32); rkTo[64:128] = rkT
    rvtp = np.zeros((128, 64), np.float32); rvtp[0:33] = np.asarray(inputs["L1_attn_rv_table"])
    im = {
        "wqT": np.ascontiguousarray(np.asarray(inputs["L1_attn_Wq"]).T).astype(np.float32),
        "wkT": np.ascontiguousarray(np.asarray(inputs["L1_attn_Wk"]).T).astype(np.float32),
        "wvT": np.ascontiguousarray(np.asarray(inputs["L1_attn_Wv"]).T).astype(np.float32),
        "woT": np.ascontiguousarray(np.asarray(inputs["L1_attn_Wo"]).T).astype(np.float32),
        "bqkv": np.concatenate([chunkcol(np.asarray(inputs["L1_attn_bq"])),
                                chunkcol(np.asarray(inputs["L1_attn_bk"])),
                                chunkcol(np.asarray(inputs["L1_attn_bv"])),
                                chunkcol(np.asarray(inputs["L1_attn_bo"]))], axis=1).astype(np.float32),
        "rkTp": np.concatenate([rkTe, rkTo], axis=1),
        "rvt": rvtp,
        "gwT": np.ascontiguousarray(np.asarray(inputs["L1_ffn_gate_w"]).T).astype(np.float32),
        "ebias": np.ascontiguousarray(np.broadcast_to(
            np.asarray(inputs["L1_ffn_expert_bias"])[None, :], (128, 8))).astype(np.float32),
        "wupT": np.ascontiguousarray(np.asarray(inputs["L1_ffn_w_up"]).transpose(0, 2, 1)).astype(np.float32),
        "wdnT": np.ascontiguousarray(np.asarray(inputs["L1_ffn_w_down"]).transpose(0, 2, 1)).astype(np.float32),
        "identf": np.eye(128, dtype=np.float32),
        "bmaskd": (np.abs(s - tp) <= 15).astype(np.float32),
        "lmaskd": (s <= tp - 16).astype(np.float32),
        "hmaskd": (s >= tp + 16).astype(np.float32),
        "pbmaskd": ((r >= 1) & (r <= 31) & (tp + r - 16 >= 0) & (tp + r - 16 <= 31)).astype(np.float32),
        "seld": np.eye(8, dtype=np.float32).repeat(128, 1).reshape(8, 8, 128).transpose(0, 1, 2).reshape(8, 1024) * 0 + np.kron(np.eye(8, dtype=np.float32), np.ones((1, 128), np.float32)),
    }
    return im


def pack_xin(x2_slice):
    import numpy as np
    NB = x2_slice.shape[1]
    return np.ascontiguousarray(
        np.asarray(x2_slice).transpose(2, 1, 0).reshape(8, 128, NB * 32)).astype(np.float32)


def unpack_xout(xo, NB):
    import numpy as np
    return np.ascontiguousarray(
        xo.reshape(8, 128, NB, 32).transpose(3, 2, 0, 1).reshape(32, NB, 1024))


import numpy as np
from concourse.bass_utils import run_bass_kernel_spmd

_NSEQ_PER_CORE = 4
_NB_PER_CORE = 256


def kernel(**inputs):
    x = np.ascontiguousarray(np.asarray(inputs["L0_attn_x"], np.float32))
    assert x.shape == (32, 2048, 1024)

    ncA, _prA = build_phase_a(NSEQ=_NSEQ_PER_CORE, do_l0attn=True, do_l0ffn=True)
    imA = host_prep_a(inputs)
    imsA = []
    for c in range(8):
        im = dict(imA)
        im["xin"] = pack_xin_a(x[c * _NSEQ_PER_CORE:(c + 1) * _NSEQ_PER_CORE])
        imsA.append(im)
    resA = run_bass_kernel_spmd(ncA, imsA, core_ids=list(range(8)))
    x2 = np.empty((32, 2048, 1024), np.float32)
    for c in range(8):
        x2[c * _NSEQ_PER_CORE:(c + 1) * _NSEQ_PER_CORE] = unpack_xout_a(
            np.asarray(resA.results[c]["xout"]))

    ncB, _prB = build_phase_b(NB=_NB_PER_CORE, do_attn=True, do_ffn=True)
    imB = host_prep_b(inputs)
    imsB = []
    for c in range(8):
        im = dict(imB)
        im["xin"] = pack_xin(x2[:, c * _NB_PER_CORE:(c + 1) * _NB_PER_CORE])
        imsB.append(im)
    resB = run_bass_kernel_spmd(ncB, imsB, core_ids=list(range(8)))
    out = np.empty((32, 2048, 1024), np.float32)
    for c in range(8):
        out[:, c * _NB_PER_CORE:(c + 1) * _NB_PER_CORE] = unpack_xout(
            np.asarray(resB.results[c]["xout"]), _NB_PER_CORE)
    return out
